# version 19
# baseline (speedup 1.0000x reference)
import math
import numpy as np
import concourse.bass as bass
import concourse.mybir as mybir
from concourse.bass_utils import run_bass_kernel_spmd

F32 = mybir.dt.float32
BF16 = mybir.dt.bfloat16
AF = mybir.ActivationFunctionType
ALU = mybir.AluOpType
AX = mybir.AxisListType

NSEQ = 2
NT = 16
NCH = 8
NH = 16
NE = 15
SEQ = 2048
D = 1024
NEG = -1.0e30


def _bucket_ref(n):
    if n < 16:
        return n
    v = np.float32(np.log(np.float32(n) / np.float32(16.0)))
    v = np.float32(v / np.float32(math.log(128.0 / 16.0)))
    v = np.float32(v * np.float32(16.0))
    return min(31, 16 + int(v))


CBS = []
for _b in range(32):
    _d = 0
    while _bucket_ref(_d) < _b:
        _d += 1
    CBS.append(_d)


class Sem:
    def __init__(self, h):
        self.h = h
        self.n = 0


class Emit:
    ENGINES = ("tensor", "vector", "scalar", "gpsimd", "sync")

    def __init__(self):
        self.q = {e: [] for e in self.ENGINES}

    def op(self, eng, meth, *a, inc=None, incn=1, **kw):
        def f(e, meth=meth, a=a, kw=kw, inc=inc, incn=incn):
            ins = getattr(e, meth)(*a, **kw)
            if inc is not None:
                ins.then_inc(inc.h, incn)
        self.q[eng].append(f)
        if inc is not None:
            inc.n += incn

    def wait(self, eng, sem, val=None):
        v = sem.n if val is None else val
        if v <= 0:
            return
        self.q[eng].append(lambda e, s=sem.h, v=v: e.wait_ge(s, v))


def build_nc(n_seq=NSEQ, upto="full", dump=None):
    nc = bass.Bass("TRN2", target_bir_lowering=False, detect_race_conditions=False)

    x_d = nc.dram_tensor("x", [NSEQ, SEQ, D], F32, kind="ExternalInput")
    wq_d = nc.dram_tensor("wq", [D, D], F32, kind="ExternalInput")
    wk_d = nc.dram_tensor("wk", [D, D], F32, kind="ExternalInput")
    wv_d = nc.dram_tensor("wv", [D, D], F32, kind="ExternalInput")
    wo_d = nc.dram_tensor("wo", [D, D], F32, kind="ExternalInput")
    bq_d = nc.dram_tensor("bq", [D], F32, kind="ExternalInput")
    bk_d = nc.dram_tensor("bk", [D], F32, kind="ExternalInput")
    bv_d = nc.dram_tensor("bv", [D], F32, kind="ExternalInput")
    bo_d = nc.dram_tensor("bo", [D], F32, kind="ExternalInput")
    re_d = nc.dram_tensor("re", [32, NH], F32, kind="ExternalInput")
    rw_d = nc.dram_tensor("rw", [D, NE], F32, kind="ExternalInput")
    rb_d = nc.dram_tensor("rb", [NE], F32, kind="ExternalInput")
    sw1_d = nc.dram_tensor("sw1", [D, D], F32, kind="ExternalInput")
    sb1_d = nc.dram_tensor("sb1", [D], F32, kind="ExternalInput")
    sw2_d = nc.dram_tensor("sw2", [D, D], F32, kind="ExternalInput")
    sb2_d = nc.dram_tensor("sb2", [D], F32, kind="ExternalInput")
    ew1_d = nc.dram_tensor("ew1", [NE, D, D], F32, kind="ExternalInput")
    eb1_d = nc.dram_tensor("eb1", [NE, D], F32, kind="ExternalInput")
    ew2_d = nc.dram_tensor("ew2", [NE, D, D], F32, kind="ExternalInput")
    eb2_d = nc.dram_tensor("eb2", [NE, D], F32, kind="ExternalInput")
    y_d = nc.dram_tensor("y", [NSEQ, SEQ, D], F32, kind="ExternalOutput")
    wts_d = nc.dram_tensor("wts", [4, 128, 8192], BF16, kind="Internal")

    x = x_d.ap()
    y = y_d.ap()

    ARENA = 211712
    arena = nc.alloc_sbuf_tensor("arena", [128, ARENA], mybir.dt.uint8)
    abase = nc.lookup_mloc(arena).addr
    _cur = [0]
    nc._layout = {}

    def sb(name, shape, dtype, off=None):
        if off is None:
            off = _cur[0]
        esz = mybir.dt.size(dtype)
        rowb = int(np.prod(shape[1:])) * esz
        _cur[0] = max(_cur[0], off + rowb)
        _cur[0] = (_cur[0] + 63) & ~63
        assert off + rowb <= ARENA, (name, off, rowb)
        assert off % esz == 0, (name, off)
        nc._layout[name] = (off, tuple(shape), dtype)
        u8 = bass.AP(arena, off, [[ARENA, shape[0]], [1, rowb]])
        return u8.bitcast(dtype)

    def mk3(view, elem_off, dims):
        esz = mybir.dt.size(view.dtype)
        byte_off = (view.offset + elem_off) * esz
        assert dims[-1][0] == 1
        u8dims = [[d[0] * esz, d[1]] for d in dims[:-1]] + [[1, dims[-1][1] * esz]]
        u8dims[0] = [ARENA, dims[0][1]]
        return bass.AP(arena, byte_off, u8dims).bitcast(view.dtype)

    identf = sb("identf", [128, 128], F32)
    P_bf = sb("P_bf", [128, NH * 256], BF16)
    PT_bf = sb("PT_bf", [128, NH * 256], BF16)
    bqh = sb("bqh", [64, 16], F32)
    bkh = sb("bkh", [64, 16], F32)
    sb1c = sb("sb1c", [128, 8], F32)
    eb1c = sb("eb1c", [128, 120], F32)
    br = [sb("br0", [1, 1024], BF16)]
    brow_rt = sb("brow_rt", [1, 16], BF16)
    rW = sb("rW", [128, 120], BF16)
    ones_bf = sb("ones_bf", [128, 128], BF16)
    onesf = sb("onesf", [1, 128], F32)
    epsc = sb("epsc", [128, 1], F32)
    negmcols = sb("negmcols", [128, 16], F32)
    negmT_sb = sb("negmT_sb", [16, 128], BF16)
    stats = sb("stats", [128, 16], F32)
    top8 = sb("top8", [128, 8], F32)
    msk15 = sb("msk15", [128, 16], F32)
    probs = sb("probs", [128, 16], F32)
    cw = sb("cw", [128, NT * NE], F32)
    re_sb = sb("re_sb", [33, 16], F32)
    prev_sb = sb("prev_sb", [33, 16], F32)
    vaug = sb("vaug", [32, 16], F32)
    eb2a = sb("eb2a", [15, 1024], BF16)
    cwT = sb("cwT", [15, SEQ], BF16)
    A0 = (_cur[0] + 255) & ~255
    x1hatT = sb("x1hatT", [128, NCH * SEQ], BF16, A0)
    x2 = sb("x2", [128, NT * 1024], F32, A0)
    B0 = A0 + 32768
    V_all = sb("V_all", [128, NT * 1040], BF16, B0)
    C0 = B0 + 33280
    wqT = sb("wqT", [128, 8192], BF16, C0)
    wkT = sb("wkT", [128, 8192], BF16, C0 + 16384)
    wvT = sb("wvT", [128, 8192], BF16, C0 + 32768)
    woT = sb("woT", [128, 8192], BF16, C0)
    wbuf = [sb("wbuf%d" % i, [128, 8192], BF16, C0 + 16384 * i) for i in range(3)]
    D0 = C0 + 49152
    Q_h = sb("Q_h", [65, SEQ], BF16, D0)
    K_h = sb("K_h", [65, SEQ], BF16, D0 + 4096)
    AT = [sb("AT0", [128, 512], BF16, D0 + 8192),
          sb("AT1", [128, 512], BF16, D0 + 9216),
          sb("AT2", [128, 512], BF16, D0 + 33280)]
    pair = sb("pair", [128, NT * 128], F32, D0 + 10240)
    xin = [sb("xin0", [128, 1024], F32, D0 + 18432),
           sb("xin1", [128, 1024], F32, D0 + 22528)]
    Q_h2 = sb("Q_h2", [65, SEQ], BF16, D0 + 18432)
    K_h2 = sb("K_h2", [65, SEQ], BF16, D0 + 22528)
    xhat = [sb("xhat0", [128, 1024], F32, D0 + 26624),
            sb("xhat1", [128, 1024], F32, D0 + 30720)]
    htil = sb("htil", [128, NCH * SEQ], BF16, D0)
    dP = sb("dP", [128, 256], F32, D0)
    E0 = D0 + 34816
    attnOT = sb("attnOT", [128, NCH * SEQ], BF16, E0)
    x2hatT = sb("x2hatT", [128, NCH * SEQ], BF16, E0)
    S_sl = sb("S_sl", [128, NH * 256], BF16, E0)
    fbufb = sb("fbufb", [NH, 512], BF16, E0 + 8192)
    miota = sb("miota", [32, 256], F32, E0 + 9216)
    M2T = sb("M2T", [32, 256], F32, E0 + 10240)
    ones32 = sb("ones32", [32, 256], F32, E0 + 11264)
    Jrev = sb("Jrev", [128, 128], BF16, E0 + 12288)
    Gsb = sb("Gsb", [128, 32], BF16, E0 + 12544)
    CBSc = sb("CBSc", [32, 1], F32, E0 + 12608)
    CBSrow = sb("CBSrow", [1, 32], F32, E0 + 12672)
    assert E0 + 32768 <= ARENA, (E0, ARENA)

    ps_a = nc.alloc_psum_tensor("ps_a", [128, 2048], F32).ap()
    ps_b = nc.alloc_psum_tensor("ps_b", [128, 2048], F32).ap()

    import contextlib
    es = contextlib.ExitStack()
    nc._kernel_es = es

    def mksem(name):
        return Sem(es.enter_context(nc.semaphore(name)))

    sIN = mksem("sIN")
    sST = mksem("sST")
    sPE = mksem("sPE")
    sDV = mksem("sDV")
    sAC = mksem("sAC")
    sGP = mksem("sGP")
    sSET = mksem("sSET")
    sNEGh = [mksem("sNEG0"), mksem("sNEG1")]
    sWD = [mksem("sWD%d" % i) for i in range(3)]
    sXI = [mksem("sXI0"), mksem("sXI1")]
    sWB = [mksem("sWB0"), mksem("sWB1")]
    sWS = mksem("sWS")
    sWT = mksem("sWT")
    sFB = mksem("sFB")
    sBR = [mksem("sBR0")]

    dump_head = {"pair0": 0, "pair1": 1}
    em = Emit()
    wbuf_free = {}
    w2_pe_tok = 0
    _dumps = []

    def maybe_dump(tag, view, rows=None):
        if dump != tag:
            return
        r = rows if rows is not None else view.shape[0]
        cols = view.shape[1]
        dbg = nc.dram_tensor("dbg%d" % len(_dumps), [r, cols], view.dtype,
                             kind="ExternalOutput").ap()
        _dumps.append(tag)
        em.wait("sync", sPE)
        em.wait("sync", sDV)
        em.wait("sync", sAC)
        em.op("sync", "dma_start", out=dbg[:, :], in_=view[0:r, :], inc=sST, incn=16)

    br_pe = {}

    def load_brow(slot, src_ap):
        em.wait("gpsimd", sPE, br_pe.get(slot, 0))
        em.op("gpsimd", "dma_start", out=br[slot][0:1, :], in_=src_ap, inc=sBR[slot], incn=16)
        return sBR[slot].n

    def barrier_tensor():
        em.wait("tensor", sDV)
        em.wait("tensor", sAC)

    def w_build(w_dram, dstT, psb, store_idx=None):
        em.wait("sync", sPE)
        em.wait("sync", sAC)
        barrier_tensor()
        ac_tok = {}
        pe_tok = {}
        for i in range(8):
            slot = pair[:, 1024 * (i % 2):1024 * (i % 2) + 1024]
            if i >= 2:
                em.wait("sync", sPE, pe_tok[i - 2])
            for k in range(4):
                em.op("sync", "dma_start", out=slot[32 * k:32 * k + 32, :],
                      in_=w_dram.ap()[128 * i + 32 * k:128 * i + 32 * k + 32, :],
                      inc=sWB[i % 2], incn=16)
            em.wait("tensor", sWB[i % 2])
            for j in range(8):
                key = (i % 2, j % 2)
                ps = psb[:, 1024 * (i % 2) + 512 * (j % 2):
                         1024 * (i % 2) + 512 * (j % 2) + 128]
                if key in ac_tok:
                    em.wait("tensor", sAC, ac_tok[key])
                em.op("tensor", "transpose", ps, slot[:, 128 * j:128 * j + 128],
                      identf[:, :], inc=sPE)
                em.wait("scalar", sPE, sPE.n)
                em.op("scalar", "copy",
                      dstT[:, 1024 * j + 128 * i:1024 * j + 128 * i + 128], ps, inc=sAC)
                ac_tok[key] = sAC.n
            pe_tok[i] = sPE.n
        if store_idx is not None:
            em.wait("sync", sAC)
            em.op("sync", "dma_start", out=wts_d.ap()[store_idx], in_=dstT[:, :],
                  inc=sWS, incn=16)


    em.op("sync", "dma_start", out=re_sb[0:32, :], in_=re_d.ap(), inc=sIN, incn=16)
    em.op("sync", "dma_start", out=bqh[:, :],
          in_=bass.AP(bq_d, 0, [[1, 64], [64, 16]]), inc=sIN, incn=16, allow_slow_non_contiguous=True)
    em.op("sync", "dma_start", out=bkh[:, :],
          in_=bass.AP(bk_d, 0, [[1, 64], [64, 16]]), inc=sIN, incn=16, allow_slow_non_contiguous=True)
    em.op("sync", "dma_start", out=sb1c[:, :],
          in_=bass.AP(sb1_d, 0, [[1, 128], [128, 8]]), inc=sIN, incn=16, allow_slow_non_contiguous=True)
    em.op("sync", "dma_start", out=eb1c[:, :],
          in_=bass.AP(eb1_d, 0, [[1, 128], [1024, 15], [128, 8]]), inc=sIN, incn=16, allow_slow_non_contiguous=True)
    tok_in_setup = sIN.n

    em.op("gpsimd", "dma_start", out=brow_rt[0:1, 0:15], in_=rb_d.ap().unsqueeze(0), inc=sSET, incn=16)
    em.op("gpsimd", "dma_start", out=eb2a[:, :], in_=eb2_d.ap(), inc=sSET, incn=16)
    em.op("gpsimd", "dma_start", out=rW[:, :],
          in_=bass.AP(rw_d, 0, [[15, 128], [1920, 8], [1, 15]]), inc=sSET, incn=16)
    tok_set_dma = sSET.n

    em.op("gpsimd", "iota", dP[:, 0:128], [[-1, 128]], base=0, channel_multiplier=1,
          allow_small_or_imprecise_dtypes=True)
    em.op("gpsimd", "iota", dP[:, 128:256], [[1, 128]], base=0, channel_multiplier=1,
          allow_small_or_imprecise_dtypes=True)
    em.op("gpsimd", "iota", miota[:, :], [[1, 256]], base=0, channel_multiplier=0,
          allow_small_or_imprecise_dtypes=True, inc=sGP)
    tok_iota = sGP.n

    em.wait("vector", sGP, tok_iota)
    em.op("vector", "tensor_single_scalar", identf[:, :], dP[:, 0:128], 0.0, ALU.is_equal)
    em.op("vector", "tensor_single_scalar", Jrev[:, :], dP[:, 128:256], 127.0,
          ALU.is_equal)
    em.op("vector", "memset", ones_bf[:, :], 1.0)
    em.op("vector", "memset", onesf[:, :], 1.0)
    em.op("vector", "memset", epsc[:, :], 1.0e-5)
    em.op("vector", "memset", ones32[:, :], 1.0)
    for b in range(32):
        em.op("vector", "memset", CBSrow[0:1, b:b + 1], float(CBS[b]))
    em.op("vector", "memset", fbufb[0:NH, 0:127], NEG)
    em.op("vector", "engine_nop", inc=sDV)
    em.wait("vector", sDV)
    tok_preset = sDV.n
    em.wait("sync", sDV, tok_preset)
    em.op("sync", "dma_start", out=CBSc[0:32, 0:1], in_=CBSrow[0:1, 0:32],
          inc=sFB, incn=16)
    tok_cbs = sFB.n
    em.wait("vector", sFB, tok_cbs)
    em.op("vector", "scalar_tensor_tensor", M2T[:, :], miota[:, :], CBSc[:, 0:1],
          ones32[:, :], ALU.is_ge, ALU.mult, inc=sDV)
    tok_m2t = sDV.n

    em.wait("sync", sIN, tok_in_setup)
    em.op("sync", "dma_start", out=prev_sb[1:32, :], in_=re_sb[0:31, :], inc=sIN, incn=16)
    em.op("sync", "dma_start", out=prev_sb[0:1, :], in_=re_sb[31:32, :], inc=sIN, incn=16)
    tok_prev = sIN.n
    em.wait("vector", sIN, tok_prev)
    em.op("vector", "tensor_sub", vaug[0:32, :], re_sb[0:32, :], prev_sb[0:32, :],
          inc=sDV)
    em.wait("vector", sDV)
    em.op("vector", "tensor_single_scalar", vaug[0:32, :], vaug[0:32, :], 0.125, ALU.mult,
          inc=sDV)
    tok_vaug = sDV.n

    em.wait("tensor", sDV, tok_vaug)
    em.op("tensor", "matmul", ps_a[:, 0:16], M2T[0:32, 0:128], vaug[0:32, :],
          start=True, stop=True)
    em.op("tensor", "matmul", ps_a[:, 512:528], M2T[0:32, 128:256], vaug[0:32, :],
          start=True, stop=True, inc=sPE)
    tok_G = sPE.n
    em.wait("vector", sPE, tok_G)
    em.op("vector", "tensor_copy", Gsb[:, 0:16], ps_a[:, 0:16])
    em.op("vector", "tensor_copy", Gsb[:, 16:32], ps_a[:, 512:528], inc=sDV)
    tok_gs = sDV.n
    em.wait("sync", sDV, tok_gs)
    for c in range(2):
        for h in range(NH):
            em.op("sync", "dma_start",
                  out=fbufb[h:h + 1, 127 + 128 * c:127 + 128 * c + 128],
                  in_=Gsb[0:128, 16 * c + h:16 * c + h + 1], inc=sFB, incn=16)
    tok_cols = sFB.n
    em.wait("sync", sFB, tok_cols)
    fb_byte = nc._layout["fbufb"][0]
    for h in range(NH):
        src = bass.AP(arena, h * ARENA + fb_byte,
                      [[ARENA, 1], [2, 128], [1, 512]]).bitcast(BF16)
        em.op("sync", "dma_start", out=S_sl[:, 256 * h:256 * h + 256], in_=src,
              allow_slow_non_contiguous=True, inc=sFB, incn=16)
    tok_slide = sFB.n

    w_build(wq_d, wqT, ps_b, store_idx=0)
    w_build(wk_d, wkT, ps_b, store_idx=1)
    w_build(wv_d, wvT, ps_b, store_idx=2)

    em.wait("tensor", sFB, tok_slide)
    tp_dv = {}
    for h in range(NH):
        par = h % 2
        ps = ps_a[:, 1024 + 512 * par:1024 + 512 * par + 256]
        em.wait("tensor", sDV, tp_dv.get(par, 0))
        em.op("tensor", "matmul", ps, Jrev[:, :], S_sl[:, 256 * h:256 * h + 256],
              start=True, stop=True, inc=sPE)
        em.wait("vector", sPE, sPE.n)
        em.op("vector", "tensor_copy", PT_bf[:, 256 * h:256 * h + 256], ps, inc=sDV)
        tp_dv[par] = sDV.n
    em.wait("tensor", sDV, sDV.n)

    for s in range(n_seq):
        ln_tp_dv = {}

        if s > 0:
            em.wait("sync", sPE)
            em.wait("sync", sWS, sWS.n)
            for (i, dstT) in ((0, wqT), (1, wkT), (2, wvT)):
                em.op("sync", "dma_start", out=dstT[:, :], in_=wts_d.ap()[i],
                      inc=sWT, incn=16)
            tok_wt = sWT.n
            em.wait("tensor", sWT, tok_wt)

        barrier_tensor()
        if s > 0:
            em.wait("sync", sAC)
            em.wait("sync", sPE)
            em.wait("vector", sST)
            em.wait("vector", sAC)
        xin_consumed = {}
        xhat_tok = {}
        for t in range(NT):
            par = t % 2
            st = 8 * par
            if t >= 2:
                em.wait("sync", sDV, xin_consumed[par])
            em.op("sync", "dma_start", out=xin[par][:, :],
                  in_=x[s, 128 * t:128 * t + 128, :], inc=sXI[par], incn=16)
            em.wait("vector", sXI[par])
            em.op("vector", "reduce_sum", stats[:, st:st + 1], xin[par][:, :], axis=AX.X,
                  negate=True, inc=sDV)
            em.wait("vector", sDV)
            em.op("vector", "tensor_single_scalar", stats[:, st + 1:st + 2],
                  stats[:, st:st + 1], 1.0 / 1024.0, ALU.mult, inc=sDV)
            em.wait("vector", sDV)
            em.op("vector", "tensor_mul", ps_b[:, 1024 * par:1024 * par + 1024],
                  xin[par][:, :], xin[par][:, :])
            em.op("vector", "reduce_sum", stats[:, st + 2:st + 3],
                  ps_b[:, 1024 * par:1024 * par + 1024], axis=AX.X, inc=sDV)
            em.wait("vector", sDV)
            em.op("vector", "tensor_mul", stats[:, st + 5:st + 6],
                  stats[:, st + 1:st + 2], stats[:, st + 1:st + 2], inc=sDV)
            em.wait("vector", sDV)
            em.op("vector", "scalar_tensor_tensor", stats[:, st + 6:st + 7],
                  stats[:, st + 2:st + 3], 1.0 / 1024.0, stats[:, st + 5:st + 6],
                  ALU.mult, ALU.subtract, inc=sDV)
            em.wait("scalar", sDV, sDV.n)
            if t >= 2:
                em.wait("scalar", sDV, xhat_tok[par])
            em.op("scalar", "activation", stats[:, st + 4:st + 5], stats[:, st + 6:st + 7],
                  AF.Sqrt, bias=epsc[:, :], scale=1.0, inc=sAC)
            em.wait("vector", sAC, sAC.n)
            em.op("vector", "reciprocal", stats[:, st + 3:st + 4], stats[:, st + 4:st + 5],
                  inc=sDV)
            em.wait("vector", sDV)
            em.op("vector", "tensor_scalar", xhat[par][:, 0:512], xin[par][:, 0:512],
                  stats[:, st + 1:st + 2], stats[:, st + 3:st + 4], ALU.add, ALU.mult,
                  inc=sDV)
            tok_xh0 = sDV.n
            em.op("vector", "tensor_scalar", xhat[par][:, 512:1024], xin[par][:, 512:1024],
                  stats[:, st + 1:st + 2], stats[:, st + 3:st + 4], ALU.add, ALU.mult,
                  inc=sDV)
            xin_consumed[par] = sDV.n
            xhat_tok[par] = sDV.n
            tok_xh1 = sDV.n
            for c in range(NCH):
                key = (par, c % 2)
                ps = ps_a[:, 1024 * par + 512 * (c % 2):1024 * par + 512 * (c % 2) + 128]
                em.wait("tensor", sDV, tok_xh0 if c < 4 else tok_xh1)
                if key in ln_tp_dv:
                    em.wait("tensor", sAC, ln_tp_dv[key])
                em.op("tensor", "transpose", ps, xhat[par][:, 128 * c:128 * c + 128],
                      identf[:, :], inc=sPE)
                em.wait("scalar", sPE, sPE.n)
                em.op("scalar", "copy",
                      x1hatT[:, SEQ * c + 128 * t:SEQ * c + 128 * t + 128], ps, inc=sAC)
                ln_tp_dv[key] = sAC.n
        em.op("vector", "memset", K_h[64:65, :], 1.0)
        em.op("vector", "memset", K_h2[64:65, :], 1.0, inc=sDV)
        tok_x1hatT = sDV.n
        if s == 0:
            maybe_dump("x1hatT", x1hatT)
            maybe_dump("statsA", stats)
            maybe_dump("xhat0d", xhat[0])
            maybe_dump("xhat1d", xhat[1])

        tok_bv = load_brow(0, bv_d.ap().unsqueeze(0))
        em.wait("tensor", sDV, tok_x1hatT)
        em.wait("tensor", sAC)
        em.wait("tensor", sSET, tok_set_dma)
        em.wait("tensor", sBR[0], tok_bv)
        vp_dv = {}
        for t in range(NT):
            par = t % 2
            psv = ps_a[:, 1024 * par:1024 * par + 1024]
            if t >= 2:
                em.wait("tensor", sDV, vp_dv[par])
            for c in range(NCH):
                lhs = x1hatT[:, SEQ * c + 128 * t:SEQ * c + 128 * t + 128]
                for w in range(2):
                    em.op("tensor", "matmul", psv[:, 512 * w:512 * w + 512],
                          lhs, wvT[:, 1024 * c + 512 * w:1024 * c + 512 * w + 512],
                          start=(c == 0), stop=False)
            for w in range(2):
                em.op("tensor", "matmul", psv[:, 512 * w:512 * w + 512],
                      ones_bf[0:1, :], br[0][0:1, 512 * w:512 * w + 512],
                      start=False, stop=True, inc=(sPE if w == 1 else None))
            em.wait("vector", sPE, sPE.n)
            em.op("vector", "tensor_copy",
                  mk3(V_all, 1040 * t, [[1040 * NT, 128], [65, 16], [1, 64]]),
                  bass.AP(psv.tensor, psv.offset, [[2048, 128], [64, 16], [1, 64]]))
            em.op("vector", "memset",
                  mk3(V_all, 1040 * t + 64, [[1040 * NT, 128], [65, 16], [1, 1]]),
                  1.0, inc=sDV)
            vp_dv[par] = sDV.n
        br_pe[0] = sPE.n
        if s == 0:
            maybe_dump("V", V_all)

        prev_head_dv = sDV.n
        prev_head_pe = sPE.n
        pair_dv = {}
        QKBUF = [(Q_h, K_h), (Q_h2, K_h2)]
        qk_ready = {}

        def emit_quarter(hh, qq):
            dst = QKBUF[hh % 2][0 if qq < 4 else 1]
            wT = wqT if qq < 4 else wkT
            bias_t = bqh if qq < 4 else bkh
            g = qq % 4
            psq = ps_a[0:64, 1024:1536]
            if c1_tok[0]:
                em.wait("tensor", sDV if c1_tok[0][0] == "dv" else sAC, c1_tok[0][1])
            for c in range(NCH):
                em.op("tensor", "matmul", psq,
                      wT[:, 1024 * c + 64 * hh:1024 * c + 64 * hh + 64],
                      x1hatT[:, SEQ * c + 512 * g:SEQ * c + 512 * g + 512],
                      start=(c == 0), stop=(c == NCH - 1),
                      inc=(sPE if c == NCH - 1 else None))
            em.wait("scalar", sPE, sPE.n)
            em.op("scalar", "activation", dst[0:64, 512 * g:512 * g + 512], psq,
                  AF.Identity, bias=bias_t[:, hh:hh + 1], scale=1.0, inc=sAC)
            c1_tok[0] = ("ac", sAC.n)

        c1_tok = [None]
        em.wait("tensor", sDV)
        for qq in range(8):
            emit_quarter(0, qq)
        qk_ready[0] = sAC.n

        for h in range(NH):
            Qh, Kh = QKBUF[h % 2]
            if s == 0 and h == 0:
                maybe_dump("q0", Qh)
                maybe_dump("k0", Kh)

            em.wait("tensor", sAC, qk_ready[h])
            em.wait("tensor", sDV, prev_head_dv)
            s_dv_tok = {}
            for qt in range(NT):
                par = qt % 2
                psS = ps_a if par == 0 else ps_b
                W = 128 * qt + 128
                if qt >= 2:
                    em.wait("tensor", sDV, s_dv_tok[par])
                lhs = Qh[0:64, 128 * qt:128 * qt + 128]
                for g in range((W + 511) // 512):
                    n = min(512, W - 512 * g)
                    em.op("tensor", "matmul", psS[:, 512 * g:512 * g + n],
                          lhs, Kh[0:64, 512 * g:512 * g + n],
                          start=True, stop=True,
                          inc=(sPE if 512 * g + n >= W else None))
                em.wait("vector", sPE, sPE.n)
                em.op("vector", "reduce_max", negmcols[:, qt:qt + 1], psS[:, 0:W],
                      axis=AX.X, negate=True, inc=sDV)
                s_dv_tok[par] = sDV.n
            tok_negm_dv = sDV.n

            def emit_negm_tp():
                em.wait("tensor", sDV, tok_negm_dv)
                if c1_tok[0]:
                    em.wait("tensor", sDV if c1_tok[0][0] == "dv" else sAC,
                            c1_tok[0][1])
                psT = ps_a[0:16, 1024:1152]
                em.op("tensor", "transpose", psT, negmcols[:, 0:16], identf[:, :],
                      inc=sPE)
                em.wait("scalar", sPE, sPE.n)
                em.op("scalar", "copy", negmT_sb[0:16, :], psT, inc=sAC)
                c1_tok[0] = ("ac", sAC.n)
                em.wait("sync", sAC, sAC.n)
                em.op("sync", "dma_start", out=Qh[64:65, :], in_=negmT_sb[0:16, :],
                      inc=sNEGh[0], incn=16)

            if s == 0 and h == 0:
                maybe_dump("negm0", negmcols)
                maybe_dump("q0r", Qh)
            wins = [(g4, ki) for g4 in range(4) for ki in range(4 * g4 + 4)]
            exp_tok = {}
            av_tok = {}
            o_evac_dv = {}

            def emit_av(p):
                g4, ki, par = p
                if ki == 0 and g4 > 0:
                    em.wait("tensor", sDV, o_evac_dv[g4 - 1])
                em.wait("tensor", sAC, exp_tok_at[(g4, ki)])
                for qt in range(max(4 * g4, ki), 4 * g4 + 4):
                    qrel = qt - 4 * g4
                    em.op("tensor", "matmul", ps_b[:, 512 * qrel:512 * qrel + 65],
                          AT[par][:, 128 * qrel:128 * qrel + 128],
                          V_all[:, 1040 * ki + 65 * h:1040 * ki + 65 * h + 65],
                          start=(ki == 0), stop=(ki == qt),
                          inc=(sPE if ki == qt else None))
                av_tok[par] = sPE.n
                if ki == 4 * g4 + 3:
                    tok_av = sPE.n
                    em.wait("vector", sPE, tok_av)
                    em.op("vector", "reciprocal", stats[:, 8:12],
                          bass.AP(ps_b.tensor, ps_b.offset + 64, [[2048, 128], [512, 4]]),
                          inc=sDV)
                    em.wait("vector", sDV)
                    for qrel in range(4):
                        qt = 4 * g4 + qrel
                        em.op("vector", "tensor_single_scalar",
                              pair[:, 128 * qt + 64 * (h % 2):128 * qt + 64 * (h % 2) + 64],
                              ps_b[:, 512 * qrel:512 * qrel + 64],
                              stats[:, 8 + qrel:9 + qrel], ALU.mult, inc=sDV)
                    o_evac_dv[g4] = sDV.n

            exp_tok_at = {}
            pendq = []
            STOFF = (0, 512, 1536)
            nq = [0]
            if h + 1 < NH:
                em.wait("tensor", sDV, s_dv_tok[0])
                emit_quarter(h + 1, 0)
                emit_quarter(h + 1, 1)
                nq[0] = 2
            emit_negm_tp()
            tok_neg = sNEGh[0].n
            if h + 1 < NH:
                emit_quarter(h + 1, 2)
                emit_quarter(h + 1, 3)
                nq[0] = 4
            for idx, (g4, ki) in enumerate(wins):
                par = idx % 3
                psW = ps_a[:, STOFF[par]:STOFF[par] + 512]
                off = max(0, 128 * ki - 512 * g4)
                n = 512 - off
                if par in exp_tok:
                    em.wait("tensor", sAC, exp_tok[par])
                if idx == 0:
                    em.wait("tensor", sNEGh[0], tok_neg)
                em.op("tensor", "matmul", psW[:, off:off + n],
                      Kh[0:65, 128 * ki:128 * ki + 128],
                      Qh[0:65, 512 * g4 + off:512 * g4 + off + n],
                      start=True, stop=True, inc=sPE)
                tok_w = sPE.n
                bias_ops = []
                if 4 * g4 <= ki <= 4 * g4 + 3:
                    bias_ops.append((128 * (ki - 4 * g4), 256 * h))
                if 4 * g4 - 1 <= ki <= 4 * g4 + 2:
                    bias_ops.append((128 * (ki + 1 - 4 * g4), 256 * h + 128))
                if bias_ops:
                    em.wait("vector", sPE, tok_w)
                    for bi, (co, po) in enumerate(bias_ops):
                        em.op("vector", "tensor_tensor", psW[:, co:co + 128],
                              psW[:, co:co + 128], PT_bf[:, po:po + 128], ALU.add,
                              inc=(sDV if bi == len(bias_ops) - 1 else None))
                    em.wait("scalar", sDV, sDV.n)
                else:
                    em.wait("scalar", sPE, tok_w)
                if par in av_tok:
                    em.wait("scalar", sPE, av_tok[par])
                em.op("scalar", "activation", AT[par][:, off:off + n],
                      psW[:, off:off + n], AF.Exp, bias=0.0, scale=1.0, inc=sAC)
                exp_tok[par] = sAC.n
                exp_tok_at[(g4, ki)] = sAC.n
                pendq.append((g4, ki, par))
                if len(pendq) > 2:
                    emit_av(pendq.pop(0))
                if idx % 5 == 4 and h + 1 < NH and nq[0] < 8:
                    emit_quarter(h + 1, nq[0])
                    nq[0] += 1
            for p in pendq:
                emit_av(p)
            if h + 1 < NH:
                while nq[0] < 8:
                    emit_quarter(h + 1, nq[0])
                    nq[0] += 1
                qk_ready[h + 1] = sAC.n
            o_evac_dv = o_evac_dv[3]

            prev_head_dv = o_evac_dv
            prev_head_pe = sPE.n

            if h % 2 == 1:
                ch = h // 2
                em.wait("tensor", sDV, o_evac_dv)
                for qt in range(NT):
                    key = qt % 4
                    ps = ps_b[:, 512 * key:512 * key + 128]
                    if key in pair_dv:
                        em.wait("tensor", sDV, pair_dv[key])
                    em.op("tensor", "transpose", ps, pair[:, 128 * qt:128 * qt + 128],
                          identf[:, :], inc=sPE)
                    em.wait("vector", sPE, sPE.n)
                    em.op("vector", "tensor_copy",
                          attnOT[:, SEQ * ch + 128 * qt:SEQ * ch + 128 * qt + 128], ps,
                          inc=sDV)
                    pair_dv[key] = sDV.n
                prev_head_dv = sDV.n
                prev_head_pe = sPE.n
        tok_attnOT = sDV.n
        if s == 0:
            maybe_dump("attnOT", attnOT)

        tok_bo = load_brow(0, bo_d.ap().unsqueeze(0))
        if s == 0:
            w_build(wo_d, woT, ps_a, store_idx=3)
        else:
            em.wait("sync", sPE, prev_head_pe)
            em.op("sync", "dma_start", out=woT[:, :], in_=wts_d.ap()[3],
                  inc=sWT, incn=16)
            em.wait("tensor", sWT, sWT.n)
        barrier_tensor()
        em.wait("tensor", sBR[0], tok_bo)
        em.wait("sync", sDV)
        d_dv = {}
        ln_tp_dv = {}
        xhat_tok = {}

        def emit_D(t):
            par = t % 2
            psy = ps_a[:, 1024 * par:1024 * par + 1024]
            if t >= 2:
                em.wait("sync", sDV, d_dv[par])
            em.op("sync", "dma_start", out=xin[par][:, :],
                  in_=x[s, 128 * t:128 * t + 128, :], inc=sXI[par], incn=16)
            tok_xr = sXI[par].n
            if t >= 2:
                em.wait("tensor", sDV, d_dv[par])
            for c in range(NCH):
                lhs = attnOT[:, SEQ * c + 128 * t:SEQ * c + 128 * t + 128]
                for w in range(2):
                    em.op("tensor", "matmul", psy[:, 512 * w:512 * w + 512],
                          lhs, woT[:, 1024 * c + 512 * w:1024 * c + 512 * w + 512],
                          start=(c == 0), stop=False)
            for w in range(2):
                em.op("tensor", "matmul", psy[:, 512 * w:512 * w + 512],
                      ones_bf[0:1, :], br[0][0:1, 512 * w:512 * w + 512],
                      start=False, stop=True, inc=(sPE if w == 1 else None))
            return tok_xr, sPE.n

        def emit_D_vec(t, tok_xr, tok_pe):
            par = t % 2
            psy = ps_a[:, 1024 * par:1024 * par + 1024]
            em.wait("vector", sPE, tok_pe)
            em.wait("vector", sXI[par], tok_xr)
            em.op("vector", "tensor_tensor", x2[:, 1024 * t:1024 * t + 1024],
                  xin[par][:, :], psy[:, :], ALU.add, inc=sDV)
            d_dv[par] = sDV.n

        def emit_E(t):
            par = t % 2
            st = 8 * par
            x2t = x2[:, 1024 * t:1024 * t + 1024]
            em.op("vector", "reduce_sum", stats[:, st:st + 1], x2t, axis=AX.X,
                  negate=True, inc=sDV)
            em.op("vector", "tensor_mul", ps_b[:, 0:1024], x2t, x2t)
            em.op("vector", "reduce_sum", stats[:, st + 2:st + 3],
                  ps_b[:, 0:1024], axis=AX.X, inc=sDV)
            em.wait("scalar", sDV, sDV.n)
            if t >= 2:
                em.wait("scalar", sDV, xhat_tok[par])
            em.op("scalar", "activation", stats[:, st + 1:st + 2], stats[:, st:st + 1],
                  AF.Identity, bias=0.0, scale=1.0 / 1024.0)
            em.op("scalar", "activation", stats[:, st + 5:st + 6], stats[:, st:st + 1],
                  AF.Square, bias=0.0, scale=1.0 / 1024.0, inc=sAC)
            em.wait("vector", sAC, sAC.n)
            em.op("vector", "scalar_tensor_tensor", stats[:, st + 6:st + 7],
                  stats[:, st + 2:st + 3], 1.0 / 1024.0, stats[:, st + 5:st + 6],
                  ALU.mult, ALU.subtract, inc=sDV)
            em.wait("scalar", sDV, sDV.n)
            em.op("scalar", "activation", stats[:, st + 4:st + 5], stats[:, st + 6:st + 7],
                  AF.Sqrt, bias=epsc[:, :], scale=1.0, inc=sAC)
            em.wait("vector", sAC, sAC.n)
            em.op("vector", "reciprocal", stats[:, st + 3:st + 4], stats[:, st + 4:st + 5],
                  inc=sDV)
            em.wait("vector", sDV)
            em.op("vector", "tensor_scalar", xhat[par][:, 0:512], x2t[:, 0:512],
                  stats[:, st + 1:st + 2], stats[:, st + 3:st + 4], ALU.add, ALU.mult,
                  inc=sDV)
            tok_xh0 = sDV.n
            em.op("vector", "tensor_scalar", xhat[par][:, 512:1024], x2t[:, 512:1024],
                  stats[:, st + 1:st + 2], stats[:, st + 3:st + 4], ALU.add, ALU.mult,
                  inc=sDV)
            xhat_tok[par] = sDV.n
            tok_xh1 = sDV.n
            for c in range(NCH):
                key = (par, c % 2)
                ps = ps_b[:, 1024 + 512 * (c % 2):1024 + 512 * (c % 2) + 128]
                em.wait("tensor", sDV, tok_xh0 if c < 4 else tok_xh1)
                if key in ln_tp_dv:
                    em.wait("tensor", sAC, ln_tp_dv[key])
                em.op("tensor", "transpose", ps, xhat[par][:, 128 * c:128 * c + 128],
                      identf[:, :], inc=sPE)
                em.wait("scalar", sPE, sPE.n)
                em.op("scalar", "copy",
                      x2hatT[:, SEQ * c + 128 * t:SEQ * c + 128 * t + 128], ps, inc=sAC)
                ln_tp_dv[key] = sAC.n

        if upto == "attn":
            for t in range(NT):
                tok_xr, tok_pe = emit_D(t)
                emit_D_vec(t, tok_xr, tok_pe)
            em.wait("sync", sDV)
            for t in range(NT):
                em.op("sync", "dma_start", out=y[s, 128 * t:128 * t + 128, :],
                      in_=x2[:, 1024 * t:1024 * t + 1024], inc=sST, incn=16)
            br_pe[1] = sPE.n
            for _i in range(3):
                wbuf_free[_i] = max(wbuf_free.get(_i, 0), sPE.n)
            continue

        for t in range(NT + 1):
            if t < NT:
                tok_xr, tok_pe = emit_D(t)
            if t >= 1:
                emit_E(t - 1)
            if t < NT:
                emit_D_vec(t, tok_xr, tok_pe)
        tok_x2 = sDV.n
        br_pe[1] = sPE.n
        for _i in range(3):
            wbuf_free[_i] = max(wbuf_free.get(_i, 0), sPE.n)
        tok_x2hatT = sAC.n

        em.wait("tensor", sAC, tok_x2hatT)
        rt_dv = {}
        cw_tok = 0
        cw_pend = []

        def emit_cwT(p):
            tt, tok = p
            parr = tt % 2
            psT2 = ps_a[0:15, 512 * parr + 256:512 * parr + 384]
            em.wait("tensor", sDV, tok)
            em.op("tensor", "transpose", psT2, cw[:, 15 * tt:15 * tt + 15],
                  identf[:, :], inc=sPE)
            em.wait("vector", sPE, sPE.n)
            em.op("vector", "tensor_copy", cwT[0:15, 128 * tt:128 * tt + 128], psT2,
                  inc=sDV)
            rt_dv[parr] = max(rt_dv.get(parr, 0), sDV.n)

        for t in range(NT):
            par = t % 2
            psl = ps_a[:, 512 * par:512 * par + 15]
            if par in rt_dv:
                em.wait("tensor", sDV, rt_dv[par])
            for c in range(NCH):
                em.op("tensor", "matmul", psl,
                      x2hatT[:, SEQ * c + 128 * t:SEQ * c + 128 * t + 128],
                      rW[:, 15 * c:15 * c + 15], start=(c == 0), stop=False)
            em.op("tensor", "matmul", psl, ones_bf[0:1, :], brow_rt[0:1, 0:15],
                  start=False, stop=True, inc=sPE)
            em.wait("vector", sPE, sPE.n)
            em.op("vector", "reduce_max", stats[:, 8:9], psl, axis=AX.X, negate=True,
                  inc=sDV)
            em.wait("scalar", sDV, sDV.n)
            if cw_tok:
                em.wait("scalar", sDV, cw_tok)
            em.op("scalar", "activation", probs[:, 0:15], psl, AF.Exp,
                  bias=stats[:, 8:9], scale=1.0, accum_out=stats[:, 9:10], inc=sAC)
            em.wait("vector", sAC, sAC.n)
            em.op("vector", "reciprocal", stats[:, 10:11], stats[:, 9:10], inc=sDV)
            em.wait("vector", sDV)
            em.op("vector", "tensor_single_scalar", probs[:, 0:15], probs[:, 0:15],
                  stats[:, 10:11], ALU.mult, inc=sDV)
            em.wait("vector", sDV)
            em.op("vector", "max", top8[:, :], probs[:, 0:15], inc=sDV)
            em.wait("vector", sDV)
            em.op("vector", "tensor_single_scalar", msk15[:, 0:15], probs[:, 0:15],
                  top8[:, 2:3], ALU.is_ge, inc=sDV)
            em.wait("vector", sDV)
            em.op("vector", "tensor_tensor", cw[:, 15 * t:15 * t + 15], msk15[:, 0:15],
                  probs[:, 0:15], ALU.mult, inc=sDV)
            rt_dv[par] = max(rt_dv.get(par, 0), sDV.n)
            cw_tok = sDV.n
            cw_pend.append((t, sDV.n))
            if len(cw_pend) > 1:
                emit_cwT(cw_pend.pop(0))
        for p in cw_pend:
            emit_cwT(p)

        n_exp = 1 if upto == "shared" else (NE + 1)
        for e in range(n_exp):
            gi = s * (NE + 1) + e
            b1 = (2 * gi) % 3
            b2 = (2 * gi + 1) % 3
            if e == 0:
                src1 = bass.AP(sw1_d, 0, [[1024, 128], [131072, 8], [1, 1024]])
                src2 = bass.AP(sw2_d, 0, [[1024, 128], [131072, 8], [1, 1024]])
            else:
                ee = e - 1
                src1 = bass.AP(ew1_d, ee * 1048576, [[1024, 128], [131072, 8], [1, 1024]])
                src2 = bass.AP(ew2_d, ee * 1048576, [[1024, 128], [131072, 8], [1, 1024]])
            for (bi, src) in ((b1, src1), (b2, src2)):
                em.wait("gpsimd", sPE, wbuf_free.get(bi, 0))
                em.op("gpsimd", "dma_start", out=wbuf[bi][:, :], in_=src,
                      inc=sWD[bi], incn=16)
            tok_w1 = sWD[b1].n
            tok_w2 = sWD[b2].n
            tok_b2row = load_brow(0, sb2_d.ap().unsqueeze(0)) if e == 0 else 0
            barrier_tensor()
            em.wait("tensor", sWD[b1], tok_w1)
            if gi > 0:
                em.wait("scalar", sPE, w2_pe_tok)
            h_evac = {}
            for j in range(NCH):
                par = j % 2
                psh = ps_a if par == 0 else ps_b
                if par in h_evac:
                    em.wait("tensor", sAC, h_evac[par])
                for c in range(NCH):
                    lhs = wbuf[b1][:, 1024 * c + 128 * j:1024 * c + 128 * j + 128]
                    for g in range(4):
                        em.op("tensor", "matmul", psh[:, 512 * g:512 * g + 512],
                              lhs, x2hatT[:, SEQ * c + 512 * g:SEQ * c + 512 * g + 512],
                              start=(c == 0), stop=(c == NCH - 1),
                              inc=(sPE if (c == NCH - 1 and g == 3) else None))
                em.wait("scalar", sPE, sPE.n)
                if e == 0:
                    bias_ap = sb1c[:, j:j + 1]
                else:
                    bias_ap = eb1c[:, 8 * (e - 1) + j:8 * (e - 1) + j + 1]
                em.op("scalar", "activation", htil[:, SEQ * j:SEQ * j + SEQ],
                      psh[:, :], AF.Relu, bias=bias_ap, scale=1.0, inc=sAC)
                h_evac[par] = sAC.n
            tok_hdone = sAC.n
            wbuf_free[b1] = 0
            em.wait("tensor", sAC, tok_hdone)
            em.wait("tensor", sWD[b2], tok_w2)
            if e == 0:
                em.wait("tensor", sBR[0], tok_b2row)
            y_dv = {}
            for t in range(NT):
                par = t % 2
                psy = ps_a[:, 1024 * par:1024 * par + 1024]
                if par in y_dv:
                    em.wait("tensor", sDV, y_dv[par])
                for c in range(NCH):
                    lhs = htil[:, SEQ * c + 128 * t:SEQ * c + 128 * t + 128]
                    last = (c == NCH - 1 and e != 0)
                    for w in range(2):
                        em.op("tensor", "matmul", psy[:, 512 * w:512 * w + 512],
                              lhs, wbuf[b2][:, 1024 * c + 512 * w:1024 * c + 512 * w + 512],
                              start=(c == 0), stop=last,
                              inc=(sPE if (last and w == 1) else None))
                if e == 0:
                    for w in range(2):
                        em.op("tensor", "matmul", psy[:, 512 * w:512 * w + 512],
                              ones_bf[0:1, :], br[0][0:1, 512 * w:512 * w + 512],
                              start=False, stop=False)
                    for w in range(2):
                        em.op("tensor", "matmul", psy[:, 512 * w:512 * w + 512],
                              cwT[0:15, 128 * t:128 * t + 128],
                              eb2a[0:15, 512 * w:512 * w + 512],
                              start=False, stop=True, inc=(sPE if w == 1 else None))
                em.wait("vector", sPE, sPE.n)
                x2t = x2[:, 1024 * t:1024 * t + 1024]
                if e == 0:
                    em.op("vector", "tensor_tensor", x2t, x2t, psy[:, :], ALU.add, inc=sDV)
                else:
                    em.op("vector", "scalar_tensor_tensor", x2t, psy[:, :],
                          cw[:, 15 * t + e - 1:15 * t + e], x2t, ALU.mult, ALU.add,
                          inc=sDV)
                y_dv[par] = sDV.n
                if e == n_exp - 1:
                    em.wait("sync", sDV, sDV.n)
                    em.op("sync", "dma_start", out=y[s, 128 * t:128 * t + 128, :],
                          in_=x2t, inc=sST, incn=16)
            w2_pe_tok = sPE.n
            wbuf_free[b1] = sPE.n
            wbuf_free[b2] = sPE.n
            if e == 0:
                br_pe[0] = sPE.n

    em.wait("sync", sST, sST.n)

    with nc.Block() as block:
        @block.tensor
        def _(e):
            for f in em.q["tensor"]:
                f(e)

        @block.vector
        def _(e):
            for f in em.q["vector"]:
                f(e)

        @block.scalar
        def _(e):
            for f in em.q["scalar"]:
                f(e)

        @block.gpsimd
        def _(e):
            for f in em.q["gpsimd"]:
                f(e)

        @block.sync
        def _(e):
            for f in em.q["sync"]:
                f(e)

    return nc


def kernel(**inputs):
    x = np.ascontiguousarray(inputs["L0_attn_x"])
    nc = build_nc()
    base = {
        "wq": np.ascontiguousarray(inputs["L0_attn_Wq"]),
        "wk": np.ascontiguousarray(inputs["L0_attn_Wk"]),
        "wv": np.ascontiguousarray(inputs["L0_attn_Wv"]),
        "wo": np.ascontiguousarray(inputs["L0_attn_Wo"]),
        "bq": np.ascontiguousarray(inputs["L0_attn_bq"]),
        "bk": np.ascontiguousarray(inputs["L0_attn_bk"]),
        "bv": np.ascontiguousarray(inputs["L0_attn_bv"]),
        "bo": np.ascontiguousarray(inputs["L0_attn_bo"]),
        "re": np.ascontiguousarray(inputs["L0_attn_rel_emb"]),
        "rw": np.ascontiguousarray(inputs["L0_ffn_router_W"]),
        "rb": np.ascontiguousarray(inputs["L0_ffn_router_b"]),
        "sw1": np.ascontiguousarray(inputs["L0_ffn_sW1"]),
        "sb1": np.ascontiguousarray(inputs["L0_ffn_sb1"]),
        "sw2": np.ascontiguousarray(inputs["L0_ffn_sW2"]),
        "sb2": np.ascontiguousarray(inputs["L0_ffn_sb2"]),
        "ew1": np.ascontiguousarray(inputs["L0_ffn_eW1"]),
        "eb1": np.ascontiguousarray(inputs["L0_ffn_eb1"]),
        "ew2": np.ascontiguousarray(inputs["L0_ffn_eW2"]),
        "eb2": np.ascontiguousarray(inputs["L0_ffn_eb2"]),
    }
    in_maps = []
    for c in range(8):
        m = dict(base)
        m["x"] = np.ascontiguousarray(x[2 * c:2 * c + 2])
        in_maps.append(m)
    res = run_bass_kernel_spmd(nc, in_maps, core_ids=list(range(8)))
    return np.concatenate([r["y"] for r in res.results], axis=0)


# version 20 (final)
# speedup vs baseline: 1.0349x; 1.0349x over previous
import math
import numpy as np
import concourse.bass as bass
import concourse.mybir as mybir
from concourse.bass_utils import run_bass_kernel_spmd

F32 = mybir.dt.float32
BF16 = mybir.dt.bfloat16
AF = mybir.ActivationFunctionType
ALU = mybir.AluOpType
AX = mybir.AxisListType

NSEQ = 2
NT = 16
NCH = 8
NH = 16
NE = 15
SEQ = 2048
D = 1024
NEG = -1.0e30


def _bucket_ref(n):
    if n < 16:
        return n
    v = np.float32(np.log(np.float32(n) / np.float32(16.0)))
    v = np.float32(v / np.float32(math.log(128.0 / 16.0)))
    v = np.float32(v * np.float32(16.0))
    return min(31, 16 + int(v))


CBS = []
for _b in range(32):
    _d = 0
    while _bucket_ref(_d) < _b:
        _d += 1
    CBS.append(_d)


class Sem:
    def __init__(self, h):
        self.h = h
        self.n = 0


class Emit:
    ENGINES = ("tensor", "vector", "scalar", "gpsimd", "sync")

    def __init__(self):
        self.q = {e: [] for e in self.ENGINES}

    def op(self, eng, meth, *a, inc=None, incn=1, **kw):
        def f(e, meth=meth, a=a, kw=kw, inc=inc, incn=incn):
            ins = getattr(e, meth)(*a, **kw)
            if inc is not None:
                ins.then_inc(inc.h, incn)
        self.q[eng].append(f)
        if inc is not None:
            inc.n += incn

    def wait(self, eng, sem, val=None):
        v = sem.n if val is None else val
        if v <= 0:
            return
        self.q[eng].append(lambda e, s=sem.h, v=v: e.wait_ge(s, v))


def build_nc(n_seq=NSEQ, upto="full", dump=None):
    nc = bass.Bass("TRN2", target_bir_lowering=False, detect_race_conditions=False)

    x_d = nc.dram_tensor("x", [NSEQ, SEQ, D], F32, kind="ExternalInput")
    wq_d = nc.dram_tensor("wq", [D, D], F32, kind="ExternalInput")
    wk_d = nc.dram_tensor("wk", [D, D], F32, kind="ExternalInput")
    wv_d = nc.dram_tensor("wv", [D, D], F32, kind="ExternalInput")
    wo_d = nc.dram_tensor("wo", [D, D], F32, kind="ExternalInput")
    bq_d = nc.dram_tensor("bq", [D], F32, kind="ExternalInput")
    bk_d = nc.dram_tensor("bk", [D], F32, kind="ExternalInput")
    bv_d = nc.dram_tensor("bv", [D], F32, kind="ExternalInput")
    bo_d = nc.dram_tensor("bo", [D], F32, kind="ExternalInput")
    re_d = nc.dram_tensor("re", [32, NH], F32, kind="ExternalInput")
    rw_d = nc.dram_tensor("rw", [D, NE], F32, kind="ExternalInput")
    rb_d = nc.dram_tensor("rb", [NE], F32, kind="ExternalInput")
    sw1_d = nc.dram_tensor("sw1", [D, D], F32, kind="ExternalInput")
    sb1_d = nc.dram_tensor("sb1", [D], F32, kind="ExternalInput")
    sw2_d = nc.dram_tensor("sw2", [D, D], F32, kind="ExternalInput")
    sb2_d = nc.dram_tensor("sb2", [D], F32, kind="ExternalInput")
    ew1_d = nc.dram_tensor("ew1", [NE, D, D], F32, kind="ExternalInput")
    eb1_d = nc.dram_tensor("eb1", [NE, D], F32, kind="ExternalInput")
    ew2_d = nc.dram_tensor("ew2", [NE, D, D], F32, kind="ExternalInput")
    eb2_d = nc.dram_tensor("eb2", [NE, D], F32, kind="ExternalInput")
    y_d = nc.dram_tensor("y", [NSEQ, SEQ, D], F32, kind="ExternalOutput")
    wts_d = nc.dram_tensor("wts", [4, 128, 8192], BF16, kind="Internal")

    x = x_d.ap()
    y = y_d.ap()

    ARENA = 211712
    arena = nc.alloc_sbuf_tensor("arena", [128, ARENA], mybir.dt.uint8)
    abase = nc.lookup_mloc(arena).addr
    _cur = [0]
    nc._layout = {}

    def sb(name, shape, dtype, off=None):
        if off is None:
            off = _cur[0]
        esz = mybir.dt.size(dtype)
        rowb = int(np.prod(shape[1:])) * esz
        _cur[0] = max(_cur[0], off + rowb)
        _cur[0] = (_cur[0] + 63) & ~63
        assert off + rowb <= ARENA, (name, off, rowb)
        assert off % esz == 0, (name, off)
        nc._layout[name] = (off, tuple(shape), dtype)
        u8 = bass.AP(arena, off, [[ARENA, shape[0]], [1, rowb]])
        return u8.bitcast(dtype)

    def mk3(view, elem_off, dims):
        esz = mybir.dt.size(view.dtype)
        byte_off = (view.offset + elem_off) * esz
        assert dims[-1][0] == 1
        u8dims = [[d[0] * esz, d[1]] for d in dims[:-1]] + [[1, dims[-1][1] * esz]]
        u8dims[0] = [ARENA, dims[0][1]]
        return bass.AP(arena, byte_off, u8dims).bitcast(view.dtype)

    identf = sb("identf", [128, 128], F32)
    P_bf = sb("P_bf", [128, NH * 256], BF16)
    PT_bf = sb("PT_bf", [128, NH * 256], BF16)
    bqh = sb("bqh", [64, 16], F32)
    bkh = sb("bkh", [64, 16], F32)
    sb1c = sb("sb1c", [128, 8], F32)
    eb1c = sb("eb1c", [128, 120], F32)
    br = [sb("br0", [1, 1024], BF16)]
    brow_rt = sb("brow_rt", [1, 16], BF16)
    rW = sb("rW", [128, 120], BF16)
    ones_bf = sb("ones_bf", [128, 128], BF16)
    onesf = sb("onesf", [1, 128], F32)
    epsc = sb("epsc", [128, 1], F32)
    negmcols = sb("negmcols", [128, 16], F32)
    negmT_sb = sb("negmT_sb", [16, 128], BF16)
    stats = sb("stats", [128, 16], F32)
    top8 = sb("top8", [128, 8], F32)
    msk15 = sb("msk15", [128, 16], F32)
    probs = sb("probs", [128, 16], F32)
    cw = sb("cw", [128, NT * NE], F32)
    re_sb = sb("re_sb", [33, 16], F32)
    prev_sb = sb("prev_sb", [33, 16], F32)
    vaug = sb("vaug", [32, 16], F32)
    eb2a = sb("eb2a", [15, 1024], BF16)
    cwT = sb("cwT", [15, SEQ], BF16)
    A0 = (_cur[0] + 255) & ~255
    x1hatT = sb("x1hatT", [128, NCH * SEQ], BF16, A0)
    x2 = sb("x2", [128, NT * 1024], F32, A0)
    B0 = A0 + 32768
    V_all = sb("V_all", [128, NT * 1040], BF16, B0)
    C0 = B0 + 33280
    wqT = sb("wqT", [128, 8192], BF16, C0)
    wkT = sb("wkT", [128, 8192], BF16, C0 + 16384)
    wvT = sb("wvT", [128, 8192], BF16, C0 + 32768)
    woT = sb("woT", [128, 8192], BF16, C0)
    wbuf = [sb("wbuf%d" % i, [128, 8192], BF16, C0 + 16384 * i) for i in range(3)]
    D0 = C0 + 49152
    Q_h = sb("Q_h", [65, SEQ], BF16, D0)
    K_h = sb("K_h", [65, SEQ], BF16, D0 + 4096)
    AT = [sb("AT0", [128, 512], BF16, D0 + 8192),
          sb("AT1", [128, 512], BF16, D0 + 9216),
          sb("AT2", [128, 512], BF16, D0 + 33280)]
    pair = sb("pair", [128, NT * 128], F32, D0 + 10240)
    xin = [sb("xin0", [128, 1024], F32, D0 + 18432),
           sb("xin1", [128, 1024], F32, D0 + 22528)]
    Q_h2 = sb("Q_h2", [65, SEQ], BF16, D0 + 18432)
    K_h2 = sb("K_h2", [65, SEQ], BF16, D0 + 22528)
    xhat = [sb("xhat0", [128, 1024], F32, D0 + 26624),
            sb("xhat1", [128, 1024], F32, D0 + 30720)]
    htil = sb("htil", [128, NCH * SEQ], BF16, D0)
    dP = sb("dP", [128, 256], F32, D0)
    E0 = D0 + 34816
    attnOT = sb("attnOT", [128, NCH * SEQ], BF16, E0)
    x2hatT = sb("x2hatT", [128, NCH * SEQ], BF16, E0)
    S_sl = sb("S_sl", [128, NH * 256], BF16, E0)
    fbufb = sb("fbufb", [NH, 512], BF16, E0 + 8192)
    miota = sb("miota", [32, 256], F32, E0 + 9216)
    M2T = sb("M2T", [32, 256], F32, E0 + 10240)
    ones32 = sb("ones32", [32, 256], F32, E0 + 11264)
    Jrev = sb("Jrev", [128, 128], BF16, E0 + 12288)
    Gsb = sb("Gsb", [128, 32], BF16, E0 + 12544)
    CBSc = sb("CBSc", [32, 1], F32, E0 + 12608)
    CBSrow = sb("CBSrow", [1, 32], F32, E0 + 12672)
    assert E0 + 32768 <= ARENA, (E0, ARENA)

    ps_a = nc.alloc_psum_tensor("ps_a", [128, 2048], F32).ap()
    ps_b = nc.alloc_psum_tensor("ps_b", [128, 2048], F32).ap()

    import contextlib
    es = contextlib.ExitStack()
    nc._kernel_es = es

    def mksem(name):
        return Sem(es.enter_context(nc.semaphore(name)))

    sIN = mksem("sIN")
    sST = mksem("sST")
    sPE = mksem("sPE")
    sDV = mksem("sDV")
    sAC = mksem("sAC")
    sGP = mksem("sGP")
    sSET = mksem("sSET")
    sNEGh = [mksem("sNEG0"), mksem("sNEG1")]
    sWD = [mksem("sWD%d" % i) for i in range(3)]
    sXI = [mksem("sXI0"), mksem("sXI1")]
    sWB = [mksem("sWB0"), mksem("sWB1")]
    sWS = mksem("sWS")
    sWT = mksem("sWT")
    sFB = mksem("sFB")
    sRE = mksem("sRE")
    sBR = [mksem("sBR0")]

    dump_head = {"pair0": 0, "pair1": 1}
    em = Emit()
    wbuf_free = {}
    w2_pe_tok = 0
    _dumps = []

    def maybe_dump(tag, view, rows=None):
        if dump != tag:
            return
        r = rows if rows is not None else view.shape[0]
        cols = view.shape[1]
        dbg = nc.dram_tensor("dbg%d" % len(_dumps), [r, cols], view.dtype,
                             kind="ExternalOutput").ap()
        _dumps.append(tag)
        em.wait("sync", sPE)
        em.wait("sync", sDV)
        em.wait("sync", sAC)
        em.op("sync", "dma_start", out=dbg[:, :], in_=view[0:r, :], inc=sST, incn=16)

    br_pe = {}

    def load_brow(slot, src_ap):
        em.wait("gpsimd", sPE, br_pe.get(slot, 0))
        em.op("gpsimd", "dma_start", out=br[slot][0:1, :], in_=src_ap, inc=sBR[slot], incn=16)
        return sBR[slot].n

    def barrier_tensor():
        em.wait("tensor", sDV)
        em.wait("tensor", sAC)

    def w_build(w_dram, dstT, psb, store_idx=None):
        em.wait("sync", sPE)
        em.wait("sync", sAC)
        barrier_tensor()
        ac_tok = {}
        pe_tok = {}
        for i in range(8):
            slot = pair[:, 1024 * (i % 2):1024 * (i % 2) + 1024]
            if i >= 2:
                em.wait("sync", sPE, pe_tok[i - 2])
            for k in range(4):
                em.op("sync", "dma_start", out=slot[32 * k:32 * k + 32, :],
                      in_=w_dram.ap()[128 * i + 32 * k:128 * i + 32 * k + 32, :],
                      inc=sWB[i % 2], incn=16)
            em.wait("tensor", sWB[i % 2])
            for j in range(8):
                key = (i % 2, j % 2)
                ps = psb[:, 1024 * (i % 2) + 512 * (j % 2):
                         1024 * (i % 2) + 512 * (j % 2) + 128]
                if key in ac_tok:
                    em.wait("tensor", sAC, ac_tok[key])
                em.op("tensor", "transpose", ps, slot[:, 128 * j:128 * j + 128],
                      identf[:, :], inc=sPE)
                em.wait("scalar", sPE, sPE.n)
                em.op("scalar", "copy",
                      dstT[:, 1024 * j + 128 * i:1024 * j + 128 * i + 128], ps, inc=sAC)
                ac_tok[key] = sAC.n
            pe_tok[i] = sPE.n
        if store_idx is not None:
            em.wait("sync", sAC)
            em.op("sync", "dma_start", out=wts_d.ap()[store_idx], in_=dstT[:, :],
                  inc=sWS, incn=16)


    em.op("sync", "dma_start", out=re_sb[0:32, :], in_=re_d.ap(), inc=sRE, incn=16)
    em.op("sync", "dma_start", out=bqh[:, :],
          in_=bass.AP(bq_d, 0, [[1, 64], [64, 16]]), inc=sIN, incn=16, allow_slow_non_contiguous=True)
    em.op("sync", "dma_start", out=bkh[:, :],
          in_=bass.AP(bk_d, 0, [[1, 64], [64, 16]]), inc=sIN, incn=16, allow_slow_non_contiguous=True)
    em.op("sync", "dma_start", out=sb1c[:, :],
          in_=bass.AP(sb1_d, 0, [[1, 128], [128, 8]]), inc=sIN, incn=16, allow_slow_non_contiguous=True)
    em.op("sync", "dma_start", out=eb1c[:, :],
          in_=bass.AP(eb1_d, 0, [[1, 128], [1024, 15], [128, 8]]), inc=sIN, incn=16, allow_slow_non_contiguous=True)
    tok_in_setup = sIN.n

    em.op("gpsimd", "dma_start", out=brow_rt[0:1, 0:15], in_=rb_d.ap().unsqueeze(0), inc=sSET, incn=16)
    em.op("gpsimd", "dma_start", out=eb2a[:, :], in_=eb2_d.ap(), inc=sSET, incn=16)
    em.op("gpsimd", "dma_start", out=rW[:, :],
          in_=bass.AP(rw_d, 0, [[15, 128], [1920, 8], [1, 15]]), inc=sSET, incn=16)
    tok_set_dma = sSET.n

    em.op("gpsimd", "iota", dP[:, 0:128], [[-1, 128]], base=0, channel_multiplier=1,
          allow_small_or_imprecise_dtypes=True)
    em.op("gpsimd", "iota", dP[:, 128:256], [[1, 128]], base=0, channel_multiplier=1,
          allow_small_or_imprecise_dtypes=True)
    em.op("gpsimd", "iota", miota[:, :], [[1, 256]], base=0, channel_multiplier=0,
          allow_small_or_imprecise_dtypes=True, inc=sGP)
    tok_iota = sGP.n

    em.wait("vector", sGP, tok_iota)
    em.op("vector", "tensor_single_scalar", identf[:, :], dP[:, 0:128], 0.0, ALU.is_equal)
    em.op("vector", "tensor_single_scalar", Jrev[:, :], dP[:, 128:256], 127.0,
          ALU.is_equal)
    em.op("vector", "memset", ones_bf[:, :], 1.0)
    em.op("vector", "memset", onesf[:, :], 1.0)
    em.op("vector", "memset", epsc[:, :], 1.0e-5)
    em.op("vector", "memset", ones32[:, :], 1.0)
    for b in range(32):
        em.op("vector", "memset", CBSrow[0:1, b:b + 1], float(CBS[b]))
    em.op("vector", "memset", fbufb[0:NH, 0:127], NEG)
    em.op("vector", "engine_nop", inc=sDV)
    em.wait("vector", sDV)
    tok_preset = sDV.n
    em.wait("sync", sDV, tok_preset)
    em.op("sync", "dma_start", out=CBSc[0:32, 0:1], in_=CBSrow[0:1, 0:32],
          inc=sFB, incn=16)
    tok_cbs = sFB.n
    em.wait("vector", sFB, tok_cbs)
    em.op("vector", "scalar_tensor_tensor", M2T[:, :], miota[:, :], CBSc[:, 0:1],
          ones32[:, :], ALU.is_ge, ALU.mult, inc=sDV)
    tok_m2t = sDV.n

    em.wait("sync", sRE, 16)
    em.op("sync", "dma_start", out=prev_sb[1:32, :], in_=re_sb[0:31, :], inc=sRE, incn=16)
    em.op("sync", "dma_start", out=prev_sb[0:1, :], in_=re_sb[31:32, :], inc=sRE, incn=16)
    tok_prev = sRE.n
    em.wait("vector", sRE, tok_prev)
    em.op("vector", "tensor_sub", vaug[0:32, :], re_sb[0:32, :], prev_sb[0:32, :],
          inc=sDV)
    em.wait("vector", sDV)
    em.op("vector", "tensor_single_scalar", vaug[0:32, :], vaug[0:32, :], 0.125, ALU.mult,
          inc=sDV)
    tok_vaug = sDV.n

    em.wait("tensor", sDV, tok_vaug)
    em.op("tensor", "matmul", ps_a[:, 0:16], M2T[0:32, 0:128], vaug[0:32, :],
          start=True, stop=True)
    em.op("tensor", "matmul", ps_a[:, 512:528], M2T[0:32, 128:256], vaug[0:32, :],
          start=True, stop=True, inc=sPE)
    tok_G = sPE.n
    em.wait("vector", sPE, tok_G)
    em.op("vector", "tensor_copy", Gsb[:, 0:16], ps_a[:, 0:16])
    em.op("vector", "tensor_copy", Gsb[:, 16:32], ps_a[:, 512:528], inc=sDV)
    tok_gs = sDV.n
    em.wait("sync", sDV, tok_gs)
    for c in range(2):
        for h in range(NH):
            em.op("sync", "dma_start",
                  out=fbufb[h:h + 1, 127 + 128 * c:127 + 128 * c + 128],
                  in_=Gsb[0:128, 16 * c + h:16 * c + h + 1], inc=sFB, incn=16)
    tok_cols = sFB.n
    em.wait("sync", sFB, tok_cols)
    fb_byte = nc._layout["fbufb"][0]
    for h in range(NH):
        src = bass.AP(arena, h * ARENA + fb_byte,
                      [[ARENA, 1], [2, 128], [1, 512]]).bitcast(BF16)
        em.op("sync", "dma_start", out=S_sl[:, 256 * h:256 * h + 256], in_=src,
              allow_slow_non_contiguous=True, inc=sFB, incn=16)
    tok_slide = sFB.n

    w_build(wq_d, wqT, ps_b, store_idx=0)
    w_build(wk_d, wkT, ps_b, store_idx=1)
    w_build(wv_d, wvT, ps_b, store_idx=2)

    em.wait("tensor", sFB, tok_slide)
    tp_dv = {}
    for h in range(NH):
        par = h % 2
        ps = ps_a[:, 1024 + 512 * par:1024 + 512 * par + 256]
        em.wait("tensor", sDV, tp_dv.get(par, 0))
        em.op("tensor", "matmul", ps, Jrev[:, :], S_sl[:, 256 * h:256 * h + 256],
              start=True, stop=True, inc=sPE)
        em.wait("vector", sPE, sPE.n)
        em.op("vector", "tensor_copy", PT_bf[:, 256 * h:256 * h + 256], ps, inc=sDV)
        tp_dv[par] = sDV.n
    em.wait("tensor", sDV, sDV.n)

    for s in range(n_seq):
        ln_tp_dv = {}

        if s > 0:
            em.wait("sync", sPE)
            em.wait("sync", sWS, sWS.n)
            for (i, dstT) in ((0, wqT), (1, wkT), (2, wvT)):
                em.op("sync", "dma_start", out=dstT[:, :], in_=wts_d.ap()[i],
                      inc=sWT, incn=16)
            tok_wt = sWT.n
            em.wait("tensor", sWT, tok_wt)

        barrier_tensor()
        if s > 0:
            em.wait("sync", sAC)
            em.wait("sync", sPE)
            em.wait("vector", sST)
            em.wait("vector", sAC)
        xin_consumed = {}
        xhat_tok = {}
        for t in range(NT):
            par = t % 2
            st = 8 * par
            if t >= 2:
                em.wait("sync", sDV, xin_consumed[par])
            em.op("sync", "dma_start", out=xin[par][:, :],
                  in_=x[s, 128 * t:128 * t + 128, :], inc=sXI[par], incn=16)
            em.wait("vector", sXI[par])
            em.op("vector", "reduce_sum", stats[:, st:st + 1], xin[par][:, :], axis=AX.X,
                  negate=True, inc=sDV)
            em.wait("vector", sDV)
            em.op("vector", "tensor_single_scalar", stats[:, st + 1:st + 2],
                  stats[:, st:st + 1], 1.0 / 1024.0, ALU.mult, inc=sDV)
            em.wait("vector", sDV)
            em.op("vector", "tensor_mul", ps_b[:, 1024 * par:1024 * par + 1024],
                  xin[par][:, :], xin[par][:, :])
            em.op("vector", "reduce_sum", stats[:, st + 2:st + 3],
                  ps_b[:, 1024 * par:1024 * par + 1024], axis=AX.X, inc=sDV)
            em.wait("vector", sDV)
            em.op("vector", "tensor_mul", stats[:, st + 5:st + 6],
                  stats[:, st + 1:st + 2], stats[:, st + 1:st + 2], inc=sDV)
            em.wait("vector", sDV)
            em.op("vector", "scalar_tensor_tensor", stats[:, st + 6:st + 7],
                  stats[:, st + 2:st + 3], 1.0 / 1024.0, stats[:, st + 5:st + 6],
                  ALU.mult, ALU.subtract, inc=sDV)
            em.wait("scalar", sDV, sDV.n)
            if t >= 2:
                em.wait("scalar", sDV, xhat_tok[par])
            em.op("scalar", "activation", stats[:, st + 4:st + 5], stats[:, st + 6:st + 7],
                  AF.Sqrt, bias=epsc[:, :], scale=1.0, inc=sAC)
            em.wait("vector", sAC, sAC.n)
            em.op("vector", "reciprocal", stats[:, st + 3:st + 4], stats[:, st + 4:st + 5],
                  inc=sDV)
            em.wait("vector", sDV)
            em.op("vector", "tensor_scalar", xhat[par][:, 0:512], xin[par][:, 0:512],
                  stats[:, st + 1:st + 2], stats[:, st + 3:st + 4], ALU.add, ALU.mult,
                  inc=sDV)
            tok_xh0 = sDV.n
            em.op("vector", "tensor_scalar", xhat[par][:, 512:1024], xin[par][:, 512:1024],
                  stats[:, st + 1:st + 2], stats[:, st + 3:st + 4], ALU.add, ALU.mult,
                  inc=sDV)
            xin_consumed[par] = sDV.n
            xhat_tok[par] = sDV.n
            tok_xh1 = sDV.n
            for c in range(NCH):
                key = (par, c % 2)
                ps = ps_a[:, 1024 * par + 512 * (c % 2):1024 * par + 512 * (c % 2) + 128]
                em.wait("tensor", sDV, tok_xh0 if c < 4 else tok_xh1)
                if key in ln_tp_dv:
                    em.wait("tensor", sAC, ln_tp_dv[key])
                em.op("tensor", "transpose", ps, xhat[par][:, 128 * c:128 * c + 128],
                      identf[:, :], inc=sPE)
                em.wait("scalar", sPE, sPE.n)
                em.op("scalar", "copy",
                      x1hatT[:, SEQ * c + 128 * t:SEQ * c + 128 * t + 128], ps, inc=sAC)
                ln_tp_dv[key] = sAC.n
        em.op("vector", "memset", K_h[64:65, :], 1.0)
        em.op("vector", "memset", K_h2[64:65, :], 1.0, inc=sDV)
        tok_x1hatT = sDV.n
        if s == 0:
            maybe_dump("x1hatT", x1hatT)
            maybe_dump("statsA", stats)
            maybe_dump("xhat0d", xhat[0])
            maybe_dump("xhat1d", xhat[1])

        tok_bv = load_brow(0, bv_d.ap().unsqueeze(0))
        em.wait("tensor", sDV, tok_x1hatT)
        em.wait("tensor", sAC)
        em.wait("tensor", sSET, tok_set_dma)
        em.wait("tensor", sBR[0], tok_bv)
        vp_dv = {}
        for t in range(NT):
            par = t % 2
            psv = ps_a[:, 1024 * par:1024 * par + 1024]
            if t >= 2:
                em.wait("tensor", sDV, vp_dv[par])
            for c in range(NCH):
                lhs = x1hatT[:, SEQ * c + 128 * t:SEQ * c + 128 * t + 128]
                for w in range(2):
                    em.op("tensor", "matmul", psv[:, 512 * w:512 * w + 512],
                          lhs, wvT[:, 1024 * c + 512 * w:1024 * c + 512 * w + 512],
                          start=(c == 0), stop=False)
            for w in range(2):
                em.op("tensor", "matmul", psv[:, 512 * w:512 * w + 512],
                      ones_bf[0:1, :], br[0][0:1, 512 * w:512 * w + 512],
                      start=False, stop=True, inc=(sPE if w == 1 else None))
            em.wait("vector", sPE, sPE.n)
            em.op("vector", "tensor_copy",
                  mk3(V_all, 1040 * t, [[1040 * NT, 128], [65, 16], [1, 64]]),
                  bass.AP(psv.tensor, psv.offset, [[2048, 128], [64, 16], [1, 64]]))
            em.op("vector", "memset",
                  mk3(V_all, 1040 * t + 64, [[1040 * NT, 128], [65, 16], [1, 1]]),
                  1.0, inc=sDV)
            vp_dv[par] = sDV.n
        br_pe[0] = sPE.n
        if s == 0:
            maybe_dump("V", V_all)

        prev_head_dv = sDV.n
        prev_head_pe = sPE.n
        pair_dv = {}
        QKBUF = [(Q_h, K_h), (Q_h2, K_h2)]
        qk_ready = {}

        def emit_quarter(hh, qq):
            dst = QKBUF[hh % 2][0 if qq < 4 else 1]
            wT = wqT if qq < 4 else wkT
            bias_t = bqh if qq < 4 else bkh
            g = qq % 4
            psq = ps_a[0:64, 1024:1536]
            if c1_tok[0]:
                em.wait("tensor", sDV if c1_tok[0][0] == "dv" else sAC, c1_tok[0][1])
            for c in range(NCH):
                em.op("tensor", "matmul", psq,
                      wT[:, 1024 * c + 64 * hh:1024 * c + 64 * hh + 64],
                      x1hatT[:, SEQ * c + 512 * g:SEQ * c + 512 * g + 512],
                      start=(c == 0), stop=(c == NCH - 1),
                      inc=(sPE if c == NCH - 1 else None))
            em.wait("scalar", sPE, sPE.n)
            em.op("scalar", "activation", dst[0:64, 512 * g:512 * g + 512], psq,
                  AF.Identity, bias=bias_t[:, hh:hh + 1], scale=1.0, inc=sAC)
            c1_tok[0] = ("ac", sAC.n)

        c1_tok = [None]
        if s == 0:
            em.wait("scalar", sIN, tok_in_setup)
        em.wait("tensor", sDV)
        for qq in range(8):
            emit_quarter(0, qq)
        qk_ready[0] = sAC.n

        for h in range(NH):
            Qh, Kh = QKBUF[h % 2]
            if s == 0 and h == 0:
                maybe_dump("q0", Qh)
                maybe_dump("k0", Kh)

            em.wait("tensor", sAC, qk_ready[h])
            em.wait("tensor", sDV, prev_head_dv)
            s_dv_tok = {}
            for qt in range(NT):
                par = qt % 2
                psS = ps_a if par == 0 else ps_b
                W = 128 * qt + 128
                if qt >= 2:
                    em.wait("tensor", sDV, s_dv_tok[par])
                lhs = Qh[0:64, 128 * qt:128 * qt + 128]
                for g in range((W + 511) // 512):
                    n = min(512, W - 512 * g)
                    em.op("tensor", "matmul", psS[:, 512 * g:512 * g + n],
                          lhs, Kh[0:64, 512 * g:512 * g + n],
                          start=True, stop=True,
                          inc=(sPE if 512 * g + n >= W else None))
                em.wait("vector", sPE, sPE.n)
                em.op("vector", "reduce_max", negmcols[:, qt:qt + 1], psS[:, 0:W],
                      axis=AX.X, negate=True, inc=sDV)
                s_dv_tok[par] = sDV.n
            tok_negm_dv = sDV.n

            def emit_negm_tp():
                em.wait("tensor", sDV, tok_negm_dv)
                if c1_tok[0]:
                    em.wait("tensor", sDV if c1_tok[0][0] == "dv" else sAC,
                            c1_tok[0][1])
                psT = ps_a[0:16, 1024:1152]
                em.op("tensor", "transpose", psT, negmcols[:, 0:16], identf[:, :],
                      inc=sPE)
                em.wait("scalar", sPE, sPE.n)
                em.op("scalar", "copy", negmT_sb[0:16, :], psT, inc=sAC)
                c1_tok[0] = ("ac", sAC.n)
                em.wait("sync", sAC, sAC.n)
                em.op("sync", "dma_start", out=Qh[64:65, :], in_=negmT_sb[0:16, :],
                      inc=sNEGh[0], incn=16)

            if s == 0 and h == 0:
                maybe_dump("negm0", negmcols)
                maybe_dump("q0r", Qh)
            wins = [(g4, ki) for g4 in range(4) for ki in range(4 * g4 + 4)]
            exp_tok = {}
            av_tok = {}
            o_evac_dv = {}

            def emit_av(p):
                g4, ki, par = p
                if ki == 0 and g4 > 0:
                    em.wait("tensor", sDV, o_evac_dv[g4 - 1])
                em.wait("tensor", sAC, exp_tok_at[(g4, ki)])
                for qt in range(max(4 * g4, ki), 4 * g4 + 4):
                    qrel = qt - 4 * g4
                    em.op("tensor", "matmul", ps_b[:, 512 * qrel:512 * qrel + 65],
                          AT[par][:, 128 * qrel:128 * qrel + 128],
                          V_all[:, 1040 * ki + 65 * h:1040 * ki + 65 * h + 65],
                          start=(ki == 0), stop=(ki == qt),
                          inc=(sPE if ki == qt else None))
                av_tok[par] = sPE.n
                if ki == 4 * g4 + 3:
                    tok_av = sPE.n
                    em.wait("vector", sPE, tok_av)
                    em.op("vector", "reciprocal", stats[:, 8:12],
                          bass.AP(ps_b.tensor, ps_b.offset + 64, [[2048, 128], [512, 4]]),
                          inc=sDV)
                    em.wait("vector", sDV)
                    for qrel in range(4):
                        qt = 4 * g4 + qrel
                        em.op("vector", "tensor_single_scalar",
                              pair[:, 128 * qt + 64 * (h % 2):128 * qt + 64 * (h % 2) + 64],
                              ps_b[:, 512 * qrel:512 * qrel + 64],
                              stats[:, 8 + qrel:9 + qrel], ALU.mult, inc=sDV)
                    o_evac_dv[g4] = sDV.n

            exp_tok_at = {}
            pendq = []
            STOFF = (0, 512, 1536)
            nq = [0]
            if h + 1 < NH:
                em.wait("tensor", sDV, s_dv_tok[0])
                emit_quarter(h + 1, 0)
                emit_quarter(h + 1, 1)
                nq[0] = 2
            emit_negm_tp()
            tok_neg = sNEGh[0].n
            if h + 1 < NH:
                emit_quarter(h + 1, 2)
                emit_quarter(h + 1, 3)
                nq[0] = 4
            for idx, (g4, ki) in enumerate(wins):
                par = idx % 3
                psW = ps_a[:, STOFF[par]:STOFF[par] + 512]
                off = max(0, 128 * ki - 512 * g4)
                n = 512 - off
                if par in exp_tok:
                    em.wait("tensor", sAC, exp_tok[par])
                if idx == 0:
                    em.wait("tensor", sNEGh[0], tok_neg)
                em.op("tensor", "matmul", psW[:, off:off + n],
                      Kh[0:65, 128 * ki:128 * ki + 128],
                      Qh[0:65, 512 * g4 + off:512 * g4 + off + n],
                      start=True, stop=True, inc=sPE)
                tok_w = sPE.n
                bias_ops = []
                if 4 * g4 <= ki <= 4 * g4 + 3:
                    bias_ops.append((128 * (ki - 4 * g4), 256 * h))
                if 4 * g4 - 1 <= ki <= 4 * g4 + 2:
                    bias_ops.append((128 * (ki + 1 - 4 * g4), 256 * h + 128))
                if bias_ops:
                    em.wait("vector", sPE, tok_w)
                    for bi, (co, po) in enumerate(bias_ops):
                        em.op("vector", "tensor_tensor", psW[:, co:co + 128],
                              psW[:, co:co + 128], PT_bf[:, po:po + 128], ALU.add,
                              inc=(sDV if bi == len(bias_ops) - 1 else None))
                    em.wait("scalar", sDV, sDV.n)
                else:
                    em.wait("scalar", sPE, tok_w)
                if par in av_tok:
                    em.wait("scalar", sPE, av_tok[par])
                em.op("scalar", "activation", AT[par][:, off:off + n],
                      psW[:, off:off + n], AF.Exp, bias=0.0, scale=1.0, inc=sAC)
                exp_tok[par] = sAC.n
                exp_tok_at[(g4, ki)] = sAC.n
                pendq.append((g4, ki, par))
                if len(pendq) > 2:
                    emit_av(pendq.pop(0))
                if idx % 5 == 4 and h + 1 < NH and nq[0] < 8:
                    emit_quarter(h + 1, nq[0])
                    nq[0] += 1
            for p in pendq:
                emit_av(p)
            if h + 1 < NH:
                while nq[0] < 8:
                    emit_quarter(h + 1, nq[0])
                    nq[0] += 1
                qk_ready[h + 1] = sAC.n
            o_evac_dv = o_evac_dv[3]

            prev_head_dv = o_evac_dv
            prev_head_pe = sPE.n

            if h % 2 == 1:
                ch = h // 2
                em.wait("tensor", sDV, o_evac_dv)
                for qt in range(NT):
                    key = qt % 4
                    ps = ps_b[:, 512 * key:512 * key + 128]
                    if key in pair_dv:
                        em.wait("tensor", sDV, pair_dv[key])
                    em.op("tensor", "transpose", ps, pair[:, 128 * qt:128 * qt + 128],
                          identf[:, :], inc=sPE)
                    em.wait("vector", sPE, sPE.n)
                    em.op("vector", "tensor_copy",
                          attnOT[:, SEQ * ch + 128 * qt:SEQ * ch + 128 * qt + 128], ps,
                          inc=sDV)
                    pair_dv[key] = sDV.n
                prev_head_dv = sDV.n
                prev_head_pe = sPE.n
        tok_attnOT = sDV.n
        if s == 0:
            maybe_dump("attnOT", attnOT)

        tok_bo = load_brow(0, bo_d.ap().unsqueeze(0))
        if s == 0:
            w_build(wo_d, woT, ps_a, store_idx=3)
        else:
            em.wait("sync", sPE, prev_head_pe)
            em.op("sync", "dma_start", out=woT[:, :], in_=wts_d.ap()[3],
                  inc=sWT, incn=16)
            em.wait("tensor", sWT, sWT.n)
        barrier_tensor()
        em.wait("tensor", sBR[0], tok_bo)
        em.wait("sync", sDV)
        d_dv = {}
        ln_tp_dv = {}
        xhat_tok = {}

        def emit_D(t):
            par = t % 2
            psy = ps_a[:, 1024 * par:1024 * par + 1024]
            if t >= 2:
                em.wait("sync", sDV, d_dv[par])
            em.op("sync", "dma_start", out=xin[par][:, :],
                  in_=x[s, 128 * t:128 * t + 128, :], inc=sXI[par], incn=16)
            tok_xr = sXI[par].n
            if t >= 2:
                em.wait("tensor", sDV, d_dv[par])
            for c in range(NCH):
                lhs = attnOT[:, SEQ * c + 128 * t:SEQ * c + 128 * t + 128]
                for w in range(2):
                    em.op("tensor", "matmul", psy[:, 512 * w:512 * w + 512],
                          lhs, woT[:, 1024 * c + 512 * w:1024 * c + 512 * w + 512],
                          start=(c == 0), stop=False)
            for w in range(2):
                em.op("tensor", "matmul", psy[:, 512 * w:512 * w + 512],
                      ones_bf[0:1, :], br[0][0:1, 512 * w:512 * w + 512],
                      start=False, stop=True, inc=(sPE if w == 1 else None))
            return tok_xr, sPE.n

        def emit_D_vec(t, tok_xr, tok_pe):
            par = t % 2
            psy = ps_a[:, 1024 * par:1024 * par + 1024]
            em.wait("vector", sPE, tok_pe)
            em.wait("vector", sXI[par], tok_xr)
            em.op("vector", "tensor_tensor", x2[:, 1024 * t:1024 * t + 1024],
                  xin[par][:, :], psy[:, :], ALU.add, inc=sDV)
            d_dv[par] = sDV.n

        def emit_E(t):
            par = t % 2
            st = 8 * par
            x2t = x2[:, 1024 * t:1024 * t + 1024]
            em.op("vector", "reduce_sum", stats[:, st:st + 1], x2t, axis=AX.X,
                  negate=True, inc=sDV)
            em.op("vector", "tensor_mul", ps_b[:, 0:1024], x2t, x2t)
            em.op("vector", "reduce_sum", stats[:, st + 2:st + 3],
                  ps_b[:, 0:1024], axis=AX.X, inc=sDV)
            em.wait("scalar", sDV, sDV.n)
            if t >= 2:
                em.wait("scalar", sDV, xhat_tok[par])
            em.op("scalar", "activation", stats[:, st + 1:st + 2], stats[:, st:st + 1],
                  AF.Identity, bias=0.0, scale=1.0 / 1024.0)
            em.op("scalar", "activation", stats[:, st + 5:st + 6], stats[:, st:st + 1],
                  AF.Square, bias=0.0, scale=1.0 / 1024.0, inc=sAC)
            em.wait("vector", sAC, sAC.n)
            em.op("vector", "scalar_tensor_tensor", stats[:, st + 6:st + 7],
                  stats[:, st + 2:st + 3], 1.0 / 1024.0, stats[:, st + 5:st + 6],
                  ALU.mult, ALU.subtract, inc=sDV)
            em.wait("scalar", sDV, sDV.n)
            em.op("scalar", "activation", stats[:, st + 4:st + 5], stats[:, st + 6:st + 7],
                  AF.Sqrt, bias=epsc[:, :], scale=1.0, inc=sAC)
            em.wait("vector", sAC, sAC.n)
            em.op("vector", "reciprocal", stats[:, st + 3:st + 4], stats[:, st + 4:st + 5],
                  inc=sDV)
            em.wait("vector", sDV)
            em.op("vector", "tensor_scalar", xhat[par][:, 0:512], x2t[:, 0:512],
                  stats[:, st + 1:st + 2], stats[:, st + 3:st + 4], ALU.add, ALU.mult,
                  inc=sDV)
            tok_xh0 = sDV.n
            em.op("vector", "tensor_scalar", xhat[par][:, 512:1024], x2t[:, 512:1024],
                  stats[:, st + 1:st + 2], stats[:, st + 3:st + 4], ALU.add, ALU.mult,
                  inc=sDV)
            xhat_tok[par] = sDV.n
            tok_xh1 = sDV.n
            for c in range(NCH):
                key = (par, c % 2)
                ps = ps_b[:, 1024 + 512 * (c % 2):1024 + 512 * (c % 2) + 128]
                em.wait("tensor", sDV, tok_xh0 if c < 4 else tok_xh1)
                if key in ln_tp_dv:
                    em.wait("tensor", sAC, ln_tp_dv[key])
                em.op("tensor", "transpose", ps, xhat[par][:, 128 * c:128 * c + 128],
                      identf[:, :], inc=sPE)
                em.wait("scalar", sPE, sPE.n)
                em.op("scalar", "copy",
                      x2hatT[:, SEQ * c + 128 * t:SEQ * c + 128 * t + 128], ps, inc=sAC)
                ln_tp_dv[key] = sAC.n

        if upto == "attn":
            for t in range(NT):
                tok_xr, tok_pe = emit_D(t)
                emit_D_vec(t, tok_xr, tok_pe)
            em.wait("sync", sDV)
            for t in range(NT):
                em.op("sync", "dma_start", out=y[s, 128 * t:128 * t + 128, :],
                      in_=x2[:, 1024 * t:1024 * t + 1024], inc=sST, incn=16)
            br_pe[1] = sPE.n
            for _i in range(3):
                wbuf_free[_i] = max(wbuf_free.get(_i, 0), sPE.n)
            continue

        for t in range(NT + 1):
            if t < NT:
                tok_xr, tok_pe = emit_D(t)
            if t >= 1:
                emit_E(t - 1)
            if t < NT:
                emit_D_vec(t, tok_xr, tok_pe)
        tok_x2 = sDV.n
        br_pe[1] = sPE.n
        for _i in range(3):
            wbuf_free[_i] = max(wbuf_free.get(_i, 0), sPE.n)
        tok_x2hatT = sAC.n

        em.wait("tensor", sAC, tok_x2hatT)
        rt_dv = {}
        cw_tok = 0
        cw_pend = []

        def emit_cwT(p):
            tt, tok = p
            parr = tt % 2
            psT2 = ps_a[0:15, 512 * parr + 256:512 * parr + 384]
            em.wait("tensor", sDV, tok)
            em.op("tensor", "transpose", psT2, cw[:, 15 * tt:15 * tt + 15],
                  identf[:, :], inc=sPE)
            em.wait("vector", sPE, sPE.n)
            em.op("vector", "tensor_copy", cwT[0:15, 128 * tt:128 * tt + 128], psT2,
                  inc=sDV)
            rt_dv[parr] = max(rt_dv.get(parr, 0), sDV.n)

        for t in range(NT):
            par = t % 2
            psl = ps_a[:, 512 * par:512 * par + 15]
            if par in rt_dv:
                em.wait("tensor", sDV, rt_dv[par])
            for c in range(NCH):
                em.op("tensor", "matmul", psl,
                      x2hatT[:, SEQ * c + 128 * t:SEQ * c + 128 * t + 128],
                      rW[:, 15 * c:15 * c + 15], start=(c == 0), stop=False)
            em.op("tensor", "matmul", psl, ones_bf[0:1, :], brow_rt[0:1, 0:15],
                  start=False, stop=True, inc=sPE)
            em.wait("vector", sPE, sPE.n)
            em.op("vector", "reduce_max", stats[:, 8:9], psl, axis=AX.X, negate=True,
                  inc=sDV)
            em.wait("scalar", sDV, sDV.n)
            if cw_tok:
                em.wait("scalar", sDV, cw_tok)
            em.op("scalar", "activation", probs[:, 0:15], psl, AF.Exp,
                  bias=stats[:, 8:9], scale=1.0, accum_out=stats[:, 9:10], inc=sAC)
            em.wait("vector", sAC, sAC.n)
            em.op("vector", "reciprocal", stats[:, 10:11], stats[:, 9:10], inc=sDV)
            em.wait("vector", sDV)
            em.op("vector", "tensor_single_scalar", probs[:, 0:15], probs[:, 0:15],
                  stats[:, 10:11], ALU.mult, inc=sDV)
            em.wait("vector", sDV)
            em.op("vector", "max", top8[:, :], probs[:, 0:15], inc=sDV)
            em.wait("vector", sDV)
            em.op("vector", "tensor_single_scalar", msk15[:, 0:15], probs[:, 0:15],
                  top8[:, 2:3], ALU.is_ge, inc=sDV)
            em.wait("vector", sDV)
            em.op("vector", "tensor_tensor", cw[:, 15 * t:15 * t + 15], msk15[:, 0:15],
                  probs[:, 0:15], ALU.mult, inc=sDV)
            rt_dv[par] = max(rt_dv.get(par, 0), sDV.n)
            cw_tok = sDV.n
            cw_pend.append((t, sDV.n))
            if len(cw_pend) > 1:
                emit_cwT(cw_pend.pop(0))
        for p in cw_pend:
            emit_cwT(p)

        n_exp = 1 if upto == "shared" else (NE + 1)
        for e in range(n_exp):
            gi = s * (NE + 1) + e
            b1 = (2 * gi) % 3
            b2 = (2 * gi + 1) % 3
            if e == 0:
                src1 = bass.AP(sw1_d, 0, [[1024, 128], [131072, 8], [1, 1024]])
                src2 = bass.AP(sw2_d, 0, [[1024, 128], [131072, 8], [1, 1024]])
            else:
                ee = e - 1
                src1 = bass.AP(ew1_d, ee * 1048576, [[1024, 128], [131072, 8], [1, 1024]])
                src2 = bass.AP(ew2_d, ee * 1048576, [[1024, 128], [131072, 8], [1, 1024]])
            for (bi, src) in ((b1, src1), (b2, src2)):
                em.wait("gpsimd", sPE, wbuf_free.get(bi, 0))
                em.op("gpsimd", "dma_start", out=wbuf[bi][:, :], in_=src,
                      inc=sWD[bi], incn=16)
            tok_w1 = sWD[b1].n
            tok_w2 = sWD[b2].n
            tok_b2row = load_brow(0, sb2_d.ap().unsqueeze(0)) if e == 0 else 0
            barrier_tensor()
            em.wait("tensor", sWD[b1], tok_w1)
            if gi > 0:
                em.wait("scalar", sPE, w2_pe_tok)
            h_evac = {}
            for j in range(NCH):
                par = j % 2
                psh = ps_a if par == 0 else ps_b
                if par in h_evac:
                    em.wait("tensor", sAC, h_evac[par])
                for c in range(NCH):
                    lhs = wbuf[b1][:, 1024 * c + 128 * j:1024 * c + 128 * j + 128]
                    for g in range(4):
                        em.op("tensor", "matmul", psh[:, 512 * g:512 * g + 512],
                              lhs, x2hatT[:, SEQ * c + 512 * g:SEQ * c + 512 * g + 512],
                              start=(c == 0), stop=(c == NCH - 1),
                              inc=(sPE if (c == NCH - 1 and g == 3) else None))
                em.wait("scalar", sPE, sPE.n)
                if e == 0:
                    bias_ap = sb1c[:, j:j + 1]
                else:
                    bias_ap = eb1c[:, 8 * (e - 1) + j:8 * (e - 1) + j + 1]
                em.op("scalar", "activation", htil[:, SEQ * j:SEQ * j + SEQ],
                      psh[:, :], AF.Relu, bias=bias_ap, scale=1.0, inc=sAC)
                h_evac[par] = sAC.n
            tok_hdone = sAC.n
            wbuf_free[b1] = 0
            em.wait("tensor", sAC, tok_hdone)
            em.wait("tensor", sWD[b2], tok_w2)
            if e == 0:
                em.wait("tensor", sBR[0], tok_b2row)
            y_dv = {}
            for t in range(NT):
                par = t % 2
                psy = ps_a[:, 1024 * par:1024 * par + 1024]
                if par in y_dv:
                    em.wait("tensor", sDV, y_dv[par])
                for c in range(NCH):
                    lhs = htil[:, SEQ * c + 128 * t:SEQ * c + 128 * t + 128]
                    last = (c == NCH - 1 and e != 0)
                    for w in range(2):
                        em.op("tensor", "matmul", psy[:, 512 * w:512 * w + 512],
                              lhs, wbuf[b2][:, 1024 * c + 512 * w:1024 * c + 512 * w + 512],
                              start=(c == 0), stop=last,
                              inc=(sPE if (last and w == 1) else None))
                if e == 0:
                    for w in range(2):
                        em.op("tensor", "matmul", psy[:, 512 * w:512 * w + 512],
                              ones_bf[0:1, :], br[0][0:1, 512 * w:512 * w + 512],
                              start=False, stop=False)
                    for w in range(2):
                        em.op("tensor", "matmul", psy[:, 512 * w:512 * w + 512],
                              cwT[0:15, 128 * t:128 * t + 128],
                              eb2a[0:15, 512 * w:512 * w + 512],
                              start=False, stop=True, inc=(sPE if w == 1 else None))
                em.wait("vector", sPE, sPE.n)
                x2t = x2[:, 1024 * t:1024 * t + 1024]
                if e == 0:
                    em.op("vector", "tensor_tensor", x2t, x2t, psy[:, :], ALU.add, inc=sDV)
                else:
                    em.op("vector", "scalar_tensor_tensor", x2t, psy[:, :],
                          cw[:, 15 * t + e - 1:15 * t + e], x2t, ALU.mult, ALU.add,
                          inc=sDV)
                y_dv[par] = sDV.n
                if e == n_exp - 1:
                    em.wait("sync", sDV, sDV.n)
                    em.op("sync", "dma_start", out=y[s, 128 * t:128 * t + 128, :],
                          in_=x2t, inc=sST, incn=16)
            w2_pe_tok = sPE.n
            wbuf_free[b1] = sPE.n
            wbuf_free[b2] = sPE.n
            if e == 0:
                br_pe[0] = sPE.n

    em.wait("sync", sST, sST.n)

    with nc.Block() as block:
        @block.tensor
        def _(e):
            for f in em.q["tensor"]:
                f(e)

        @block.vector
        def _(e):
            for f in em.q["vector"]:
                f(e)

        @block.scalar
        def _(e):
            for f in em.q["scalar"]:
                f(e)

        @block.gpsimd
        def _(e):
            for f in em.q["gpsimd"]:
                f(e)

        @block.sync
        def _(e):
            for f in em.q["sync"]:
                f(e)

    return nc


def kernel(**inputs):
    x = np.ascontiguousarray(inputs["L0_attn_x"])
    nc = build_nc()
    base = {
        "wq": np.ascontiguousarray(inputs["L0_attn_Wq"]),
        "wk": np.ascontiguousarray(inputs["L0_attn_Wk"]),
        "wv": np.ascontiguousarray(inputs["L0_attn_Wv"]),
        "wo": np.ascontiguousarray(inputs["L0_attn_Wo"]),
        "bq": np.ascontiguousarray(inputs["L0_attn_bq"]),
        "bk": np.ascontiguousarray(inputs["L0_attn_bk"]),
        "bv": np.ascontiguousarray(inputs["L0_attn_bv"]),
        "bo": np.ascontiguousarray(inputs["L0_attn_bo"]),
        "re": np.ascontiguousarray(inputs["L0_attn_rel_emb"]),
        "rw": np.ascontiguousarray(inputs["L0_ffn_router_W"]),
        "rb": np.ascontiguousarray(inputs["L0_ffn_router_b"]),
        "sw1": np.ascontiguousarray(inputs["L0_ffn_sW1"]),
        "sb1": np.ascontiguousarray(inputs["L0_ffn_sb1"]),
        "sw2": np.ascontiguousarray(inputs["L0_ffn_sW2"]),
        "sb2": np.ascontiguousarray(inputs["L0_ffn_sb2"]),
        "ew1": np.ascontiguousarray(inputs["L0_ffn_eW1"]),
        "eb1": np.ascontiguousarray(inputs["L0_ffn_eb1"]),
        "ew2": np.ascontiguousarray(inputs["L0_ffn_eW2"]),
        "eb2": np.ascontiguousarray(inputs["L0_ffn_eb2"]),
    }
    in_maps = []
    for c in range(8):
        m = dict(base)
        m["x"] = np.ascontiguousarray(x[2 * c:2 * c + 2])
        in_maps.append(m)
    res = run_bass_kernel_spmd(nc, in_maps, core_ids=list(range(8)))
    return np.concatenate([r["y"] for r in res.results], axis=0)


# version 21
# speedup vs baseline: 1.0677x; 1.0317x over previous
import math
import numpy as np
import concourse.bass as bass
import concourse.mybir as mybir
from concourse.bass_utils import run_bass_kernel_spmd

F32 = mybir.dt.float32
BF16 = mybir.dt.bfloat16
AF = mybir.ActivationFunctionType
ALU = mybir.AluOpType
AX = mybir.AxisListType

NSEQ = 2
NT = 16
NCH = 8
NH = 16
NE = 15
SEQ = 2048
D = 1024
NEG = -1.0e30


def _bucket_ref(n):
    if n < 16:
        return n
    v = np.float32(np.log(np.float32(n) / np.float32(16.0)))
    v = np.float32(v / np.float32(math.log(128.0 / 16.0)))
    v = np.float32(v * np.float32(16.0))
    return min(31, 16 + int(v))


CBS = []
for _b in range(32):
    _d = 0
    while _bucket_ref(_d) < _b:
        _d += 1
    CBS.append(_d)


class Sem:
    def __init__(self, h):
        self.h = h
        self.n = 0


class Emit:
    ENGINES = ("tensor", "vector", "scalar", "gpsimd", "sync")

    def __init__(self):
        self.q = {e: [] for e in self.ENGINES}

    def op(self, eng, meth, *a, inc=None, incn=1, **kw):
        def f(e, meth=meth, a=a, kw=kw, inc=inc, incn=incn):
            ins = getattr(e, meth)(*a, **kw)
            if inc is not None:
                ins.then_inc(inc.h, incn)
        self.q[eng].append(f)
        if inc is not None:
            inc.n += incn

    def wait(self, eng, sem, val=None):
        v = sem.n if val is None else val
        if v <= 0:
            return
        self.q[eng].append(lambda e, s=sem.h, v=v: e.wait_ge(s, v))


def build_nc(n_seq=NSEQ, upto="full", dump=None):
    nc = bass.Bass("TRN2", target_bir_lowering=False, detect_race_conditions=False)

    x_d = nc.dram_tensor("x", [NSEQ, SEQ, D], F32, kind="ExternalInput")
    wq_d = nc.dram_tensor("wq", [D, D], F32, kind="ExternalInput")
    wk_d = nc.dram_tensor("wk", [D, D], F32, kind="ExternalInput")
    wv_d = nc.dram_tensor("wv", [D, D], F32, kind="ExternalInput")
    wo_d = nc.dram_tensor("wo", [D, D], F32, kind="ExternalInput")
    bq_d = nc.dram_tensor("bq", [D], F32, kind="ExternalInput")
    bk_d = nc.dram_tensor("bk", [D], F32, kind="ExternalInput")
    bv_d = nc.dram_tensor("bv", [D], F32, kind="ExternalInput")
    bo_d = nc.dram_tensor("bo", [D], F32, kind="ExternalInput")
    re_d = nc.dram_tensor("re", [32, NH], F32, kind="ExternalInput")
    rw_d = nc.dram_tensor("rw", [D, NE], F32, kind="ExternalInput")
    rb_d = nc.dram_tensor("rb", [NE], F32, kind="ExternalInput")
    sw1_d = nc.dram_tensor("sw1", [D, D], F32, kind="ExternalInput")
    sb1_d = nc.dram_tensor("sb1", [D], F32, kind="ExternalInput")
    sw2_d = nc.dram_tensor("sw2", [D, D], F32, kind="ExternalInput")
    sb2_d = nc.dram_tensor("sb2", [D], F32, kind="ExternalInput")
    ew1_d = nc.dram_tensor("ew1", [NE, D, D], F32, kind="ExternalInput")
    eb1_d = nc.dram_tensor("eb1", [NE, D], F32, kind="ExternalInput")
    ew2_d = nc.dram_tensor("ew2", [NE, D, D], F32, kind="ExternalInput")
    eb2_d = nc.dram_tensor("eb2", [NE, D], F32, kind="ExternalInput")
    y_d = nc.dram_tensor("y", [NSEQ, SEQ, D], F32, kind="ExternalOutput")
    wts_d = nc.dram_tensor("wts", [4, 128, 8192], BF16, kind="Internal")

    x = x_d.ap()
    y = y_d.ap()

    ARENA = 211712
    arena = nc.alloc_sbuf_tensor("arena", [128, ARENA], mybir.dt.uint8)
    abase = nc.lookup_mloc(arena).addr
    _cur = [0]
    nc._layout = {}

    def sb(name, shape, dtype, off=None):
        if off is None:
            off = _cur[0]
        esz = mybir.dt.size(dtype)
        rowb = int(np.prod(shape[1:])) * esz
        _cur[0] = max(_cur[0], off + rowb)
        _cur[0] = (_cur[0] + 63) & ~63
        assert off + rowb <= ARENA, (name, off, rowb)
        assert off % esz == 0, (name, off)
        nc._layout[name] = (off, tuple(shape), dtype)
        u8 = bass.AP(arena, off, [[ARENA, shape[0]], [1, rowb]])
        return u8.bitcast(dtype)

    def mk3(view, elem_off, dims):
        esz = mybir.dt.size(view.dtype)
        byte_off = (view.offset + elem_off) * esz
        assert dims[-1][0] == 1
        u8dims = [[d[0] * esz, d[1]] for d in dims[:-1]] + [[1, dims[-1][1] * esz]]
        u8dims[0] = [ARENA, dims[0][1]]
        return bass.AP(arena, byte_off, u8dims).bitcast(view.dtype)

    identf = sb("identf", [128, 128], F32)
    P_bf = sb("P_bf", [128, NH * 256], BF16)
    PT_bf = sb("PT_bf", [128, NH * 256], BF16)
    bqh = sb("bqh", [64, 16], F32)
    bkh = sb("bkh", [64, 16], F32)
    sb1c = sb("sb1c", [128, 8], F32)
    eb1c = sb("eb1c", [128, 120], F32)
    br = [sb("br0", [1, 1024], BF16)]
    brow_rt = sb("brow_rt", [1, 16], BF16)
    rW = sb("rW", [128, 120], BF16)
    ones_bf = sb("ones_bf", [128, 128], BF16)
    onesf = sb("onesf", [1, 128], F32)
    epsc = sb("epsc", [128, 1], F32)
    negmcols = sb("negmcols", [128, 16], F32)
    negmT_sb = sb("negmT_sb", [16, 128], BF16)
    stats = sb("stats", [128, 16], F32)
    top8 = sb("top8", [128, 8], F32)
    msk15 = sb("msk15", [128, 16], F32)
    probs = sb("probs", [128, 16], F32)
    cw = sb("cw", [128, NT * NE], F32)
    re_sb = sb("re_sb", [33, 16], F32)
    prev_sb = sb("prev_sb", [33, 16], F32)
    vaug = sb("vaug", [32, 16], F32)
    eb2a = sb("eb2a", [15, 1024], BF16)
    cwT = sb("cwT", [15, SEQ], BF16)
    A0 = (_cur[0] + 255) & ~255
    x1hatT = sb("x1hatT", [128, NCH * SEQ], BF16, A0)
    x2 = sb("x2", [128, NT * 1024], F32, A0)
    B0 = A0 + 32768
    V_all = sb("V_all", [128, NT * 1040], BF16, B0)
    C0 = B0 + 33280
    wqT = sb("wqT", [128, 8192], BF16, C0)
    wkT = sb("wkT", [128, 8192], BF16, C0 + 16384)
    wvT = sb("wvT", [128, 8192], BF16, C0 + 32768)
    woT = sb("woT", [128, 8192], BF16, C0)
    wbuf = [sb("wbuf%d" % i, [128, 8192], BF16, C0 + 16384 * i) for i in range(3)]
    D0 = C0 + 49152
    Q_h = sb("Q_h", [65, SEQ], BF16, D0)
    K_h = sb("K_h", [65, SEQ], BF16, D0 + 4096)
    AT = [sb("AT0", [128, 512], BF16, D0 + 8192),
          sb("AT1", [128, 512], BF16, D0 + 9216),
          sb("AT2", [128, 512], BF16, D0 + 33280),
          sb("AT3", [128, 512], BF16, D0 + 26624)]
    pair = sb("pair", [128, NT * 128], F32, D0 + 10240)
    xin = [sb("xin0", [128, 1024], F32, D0 + 18432),
           sb("xin1", [128, 1024], F32, D0 + 22528)]
    Q_h2 = sb("Q_h2", [65, SEQ], BF16, D0 + 18432)
    K_h2 = sb("K_h2", [65, SEQ], BF16, D0 + 22528)
    xhat = [sb("xhat0", [128, 1024], F32, D0 + 26624),
            sb("xhat1", [128, 1024], F32, D0 + 30720)]
    htil = sb("htil", [128, NCH * SEQ], BF16, D0)
    dP = sb("dP", [128, 256], F32, D0)
    E0 = D0 + 34816
    attnOT = sb("attnOT", [128, NCH * SEQ], BF16, E0)
    x2hatT = sb("x2hatT", [128, NCH * SEQ], BF16, E0)
    S_sl = sb("S_sl", [128, NH * 256], BF16, E0)
    fbufb = sb("fbufb", [NH, 512], BF16, E0 + 8192)
    miota = sb("miota", [32, 256], F32, E0 + 9216)
    M2T = sb("M2T", [32, 256], F32, E0 + 10240)
    ones32 = sb("ones32", [32, 256], F32, E0 + 11264)
    Jrev = sb("Jrev", [128, 128], BF16, E0 + 12288)
    Gsb = sb("Gsb", [128, 32], BF16, E0 + 12544)
    CBSc = sb("CBSc", [32, 1], F32, E0 + 12608)
    CBSrow = sb("CBSrow", [1, 32], F32, E0 + 12672)
    assert E0 + 32768 <= ARENA, (E0, ARENA)

    ps_a = nc.alloc_psum_tensor("ps_a", [128, 2048], F32).ap()
    ps_b = nc.alloc_psum_tensor("ps_b", [128, 2048], F32).ap()

    import contextlib
    es = contextlib.ExitStack()
    nc._kernel_es = es

    def mksem(name):
        return Sem(es.enter_context(nc.semaphore(name)))

    sIN = mksem("sIN")
    sST = mksem("sST")
    sPE = mksem("sPE")
    sDV = mksem("sDV")
    sAC = mksem("sAC")
    sGP = mksem("sGP")
    sSET = mksem("sSET")
    sNEGh = [mksem("sNEG0"), mksem("sNEG1")]
    sWD = [mksem("sWD%d" % i) for i in range(3)]
    sXI = [mksem("sXI0"), mksem("sXI1")]
    sWB = [mksem("sWB0"), mksem("sWB1")]
    sWS = mksem("sWS")
    sWT = mksem("sWT")
    sFB = mksem("sFB")
    sRE = mksem("sRE")
    sBR = [mksem("sBR0")]

    dump_head = {"pair0": 0, "pair1": 1}
    em = Emit()
    wbuf_free = {}
    w2_pe_tok = 0
    _dumps = []

    def maybe_dump(tag, view, rows=None):
        if dump != tag:
            return
        r = rows if rows is not None else view.shape[0]
        cols = view.shape[1]
        dbg = nc.dram_tensor("dbg%d" % len(_dumps), [r, cols], view.dtype,
                             kind="ExternalOutput").ap()
        _dumps.append(tag)
        em.wait("sync", sPE)
        em.wait("sync", sDV)
        em.wait("sync", sAC)
        em.op("sync", "dma_start", out=dbg[:, :], in_=view[0:r, :], inc=sST, incn=16)

    br_pe = {}

    def load_brow(slot, src_ap):
        em.wait("gpsimd", sPE, br_pe.get(slot, 0))
        em.op("gpsimd", "dma_start", out=br[slot][0:1, :], in_=src_ap, inc=sBR[slot], incn=16)
        return sBR[slot].n

    def barrier_tensor():
        em.wait("tensor", sDV)
        em.wait("tensor", sAC)

    def w_build(w_dram, dstT, psb, store_idx=None):
        em.wait("sync", sPE)
        em.wait("sync", sAC)
        barrier_tensor()
        ac_tok = {}
        pe_tok = {}
        for i in range(8):
            slot = pair[:, 1024 * (i % 2):1024 * (i % 2) + 1024]
            if i >= 2:
                em.wait("sync", sPE, pe_tok[i - 2])
            for k in range(4):
                em.op("sync", "dma_start", out=slot[32 * k:32 * k + 32, :],
                      in_=w_dram.ap()[128 * i + 32 * k:128 * i + 32 * k + 32, :],
                      inc=sWB[i % 2], incn=16)
            em.wait("tensor", sWB[i % 2])
            for j in range(8):
                key = (i % 2, j % 2)
                ps = psb[:, 1024 * (i % 2) + 512 * (j % 2):
                         1024 * (i % 2) + 512 * (j % 2) + 128]
                if key in ac_tok:
                    em.wait("tensor", sAC, ac_tok[key])
                em.op("tensor", "transpose", ps, slot[:, 128 * j:128 * j + 128],
                      identf[:, :], inc=sPE)
                em.wait("scalar", sPE, sPE.n)
                em.op("scalar", "copy",
                      dstT[:, 1024 * j + 128 * i:1024 * j + 128 * i + 128], ps, inc=sAC)
                ac_tok[key] = sAC.n
            pe_tok[i] = sPE.n
        if store_idx is not None:
            em.wait("sync", sAC)
            em.op("sync", "dma_start", out=wts_d.ap()[store_idx], in_=dstT[:, :],
                  inc=sWS, incn=16)


    em.op("sync", "dma_start", out=re_sb[0:32, :], in_=re_d.ap(), inc=sRE, incn=16)
    em.op("sync", "dma_start", out=bqh[:, :],
          in_=bass.AP(bq_d, 0, [[1, 64], [64, 16]]), inc=sIN, incn=16, allow_slow_non_contiguous=True)
    em.op("sync", "dma_start", out=bkh[:, :],
          in_=bass.AP(bk_d, 0, [[1, 64], [64, 16]]), inc=sIN, incn=16, allow_slow_non_contiguous=True)
    em.op("sync", "dma_start", out=sb1c[:, :],
          in_=bass.AP(sb1_d, 0, [[1, 128], [128, 8]]), inc=sIN, incn=16, allow_slow_non_contiguous=True)
    em.op("sync", "dma_start", out=eb1c[:, :],
          in_=bass.AP(eb1_d, 0, [[1, 128], [1024, 15], [128, 8]]), inc=sIN, incn=16, allow_slow_non_contiguous=True)
    tok_in_setup = sIN.n

    em.op("gpsimd", "dma_start", out=brow_rt[0:1, 0:15], in_=rb_d.ap().unsqueeze(0), inc=sSET, incn=16)
    em.op("gpsimd", "dma_start", out=eb2a[:, :], in_=eb2_d.ap(), inc=sSET, incn=16)
    em.op("gpsimd", "dma_start", out=rW[:, :],
          in_=bass.AP(rw_d, 0, [[15, 128], [1920, 8], [1, 15]]), inc=sSET, incn=16)
    tok_set_dma = sSET.n

    em.op("gpsimd", "iota", dP[:, 0:128], [[-1, 128]], base=0, channel_multiplier=1,
          allow_small_or_imprecise_dtypes=True)
    em.op("gpsimd", "iota", dP[:, 128:256], [[1, 128]], base=0, channel_multiplier=1,
          allow_small_or_imprecise_dtypes=True)
    em.op("gpsimd", "iota", miota[:, :], [[1, 256]], base=0, channel_multiplier=0,
          allow_small_or_imprecise_dtypes=True, inc=sGP)
    tok_iota = sGP.n

    em.wait("vector", sGP, tok_iota)
    em.op("vector", "tensor_single_scalar", identf[:, :], dP[:, 0:128], 0.0, ALU.is_equal)
    em.op("vector", "tensor_single_scalar", Jrev[:, :], dP[:, 128:256], 127.0,
          ALU.is_equal)
    em.op("vector", "memset", ones_bf[:, :], 1.0)
    em.op("vector", "memset", onesf[:, :], 1.0)
    em.op("vector", "memset", epsc[:, :], 1.0e-5)
    em.op("vector", "memset", ones32[:, :], 1.0)
    for b in range(32):
        em.op("vector", "memset", CBSrow[0:1, b:b + 1], float(CBS[b]))
    em.op("vector", "memset", fbufb[0:NH, 0:127], NEG)
    em.op("vector", "engine_nop", inc=sDV)
    em.wait("vector", sDV)
    tok_preset = sDV.n
    em.wait("sync", sDV, tok_preset)
    em.op("sync", "dma_start", out=CBSc[0:32, 0:1], in_=CBSrow[0:1, 0:32],
          inc=sFB, incn=16)
    tok_cbs = sFB.n
    em.wait("vector", sFB, tok_cbs)
    em.op("vector", "scalar_tensor_tensor", M2T[:, :], miota[:, :], CBSc[:, 0:1],
          ones32[:, :], ALU.is_ge, ALU.mult, inc=sDV)
    tok_m2t = sDV.n

    em.wait("sync", sRE, 16)
    em.op("sync", "dma_start", out=prev_sb[1:32, :], in_=re_sb[0:31, :], inc=sRE, incn=16)
    em.op("sync", "dma_start", out=prev_sb[0:1, :], in_=re_sb[31:32, :], inc=sRE, incn=16)
    tok_prev = sRE.n
    em.wait("vector", sRE, tok_prev)
    em.op("vector", "tensor_sub", vaug[0:32, :], re_sb[0:32, :], prev_sb[0:32, :],
          inc=sDV)
    em.wait("vector", sDV)
    em.op("vector", "tensor_single_scalar", vaug[0:32, :], vaug[0:32, :], 0.125, ALU.mult,
          inc=sDV)
    tok_vaug = sDV.n

    em.wait("tensor", sDV, tok_vaug)
    em.op("tensor", "matmul", ps_a[:, 0:16], M2T[0:32, 0:128], vaug[0:32, :],
          start=True, stop=True)
    em.op("tensor", "matmul", ps_a[:, 512:528], M2T[0:32, 128:256], vaug[0:32, :],
          start=True, stop=True, inc=sPE)
    tok_G = sPE.n
    em.wait("vector", sPE, tok_G)
    em.op("vector", "tensor_copy", Gsb[:, 0:16], ps_a[:, 0:16])
    em.op("vector", "tensor_copy", Gsb[:, 16:32], ps_a[:, 512:528], inc=sDV)
    tok_gs = sDV.n
    em.wait("sync", sDV, tok_gs)
    for c in range(2):
        for h in range(NH):
            em.op("sync", "dma_start",
                  out=fbufb[h:h + 1, 127 + 128 * c:127 + 128 * c + 128],
                  in_=Gsb[0:128, 16 * c + h:16 * c + h + 1], inc=sFB, incn=16)
    tok_cols = sFB.n
    em.wait("sync", sFB, tok_cols)
    fb_byte = nc._layout["fbufb"][0]
    for h in range(NH):
        src = bass.AP(arena, h * ARENA + fb_byte,
                      [[ARENA, 1], [2, 128], [1, 512]]).bitcast(BF16)
        em.op("sync", "dma_start", out=S_sl[:, 256 * h:256 * h + 256], in_=src,
              allow_slow_non_contiguous=True, inc=sFB, incn=16)
    tok_slide = sFB.n

    w_build(wq_d, wqT, ps_b, store_idx=0)
    w_build(wk_d, wkT, ps_b, store_idx=1)
    w_build(wv_d, wvT, ps_b, store_idx=2)

    em.wait("tensor", sFB, tok_slide)
    tp_dv = {}
    for h in range(NH):
        par = h % 2
        ps = ps_a[:, 1024 + 512 * par:1024 + 512 * par + 256]
        em.wait("tensor", sDV, tp_dv.get(par, 0))
        em.op("tensor", "matmul", ps, Jrev[:, :], S_sl[:, 256 * h:256 * h + 256],
              start=True, stop=True, inc=sPE)
        em.wait("vector", sPE, sPE.n)
        em.op("vector", "tensor_copy", PT_bf[:, 256 * h:256 * h + 256], ps, inc=sDV)
        tp_dv[par] = sDV.n
    em.wait("tensor", sDV, sDV.n)

    for s in range(n_seq):
        ln_tp_dv = {}

        if s > 0:
            em.wait("sync", sPE)
            em.wait("sync", sWS, sWS.n)
            for (i, dstT) in ((0, wqT), (1, wkT), (2, wvT)):
                em.op("sync", "dma_start", out=dstT[:, :], in_=wts_d.ap()[i],
                      inc=sWT, incn=16)
            tok_wt = sWT.n
            em.wait("tensor", sWT, tok_wt)

        barrier_tensor()
        if s > 0:
            em.wait("sync", sAC)
            em.wait("sync", sPE)
            em.wait("vector", sST)
            em.wait("vector", sAC)
        xin_consumed = {}
        xhat_tok = {}
        for t in range(NT):
            par = t % 2
            st = 8 * par
            if t >= 2:
                em.wait("sync", sDV, xin_consumed[par])
            em.op("sync", "dma_start", out=xin[par][:, :],
                  in_=x[s, 128 * t:128 * t + 128, :], inc=sXI[par], incn=16)
            em.wait("vector", sXI[par])
            em.op("vector", "reduce_sum", stats[:, st:st + 1], xin[par][:, :], axis=AX.X,
                  negate=True, inc=sDV)
            em.wait("vector", sDV)
            em.op("vector", "tensor_single_scalar", stats[:, st + 1:st + 2],
                  stats[:, st:st + 1], 1.0 / 1024.0, ALU.mult, inc=sDV)
            em.wait("vector", sDV)
            em.op("vector", "tensor_mul", ps_b[:, 1024 * par:1024 * par + 1024],
                  xin[par][:, :], xin[par][:, :])
            em.op("vector", "reduce_sum", stats[:, st + 2:st + 3],
                  ps_b[:, 1024 * par:1024 * par + 1024], axis=AX.X, inc=sDV)
            em.wait("vector", sDV)
            em.op("vector", "tensor_mul", stats[:, st + 5:st + 6],
                  stats[:, st + 1:st + 2], stats[:, st + 1:st + 2], inc=sDV)
            em.wait("vector", sDV)
            em.op("vector", "scalar_tensor_tensor", stats[:, st + 6:st + 7],
                  stats[:, st + 2:st + 3], 1.0 / 1024.0, stats[:, st + 5:st + 6],
                  ALU.mult, ALU.subtract, inc=sDV)
            em.wait("scalar", sDV, sDV.n)
            if t >= 2:
                em.wait("scalar", sDV, xhat_tok[par])
            em.op("scalar", "activation", stats[:, st + 4:st + 5], stats[:, st + 6:st + 7],
                  AF.Sqrt, bias=epsc[:, :], scale=1.0, inc=sAC)
            em.wait("vector", sAC, sAC.n)
            em.op("vector", "reciprocal", stats[:, st + 3:st + 4], stats[:, st + 4:st + 5],
                  inc=sDV)
            em.wait("vector", sDV)
            em.op("vector", "tensor_scalar", xhat[par][:, 0:512], xin[par][:, 0:512],
                  stats[:, st + 1:st + 2], stats[:, st + 3:st + 4], ALU.add, ALU.mult,
                  inc=sDV)
            tok_xh0 = sDV.n
            em.op("vector", "tensor_scalar", xhat[par][:, 512:1024], xin[par][:, 512:1024],
                  stats[:, st + 1:st + 2], stats[:, st + 3:st + 4], ALU.add, ALU.mult,
                  inc=sDV)
            xin_consumed[par] = sDV.n
            xhat_tok[par] = sDV.n
            tok_xh1 = sDV.n
            for c in range(NCH):
                key = (par, c % 2)
                ps = ps_a[:, 1024 * par + 512 * (c % 2):1024 * par + 512 * (c % 2) + 128]
                em.wait("tensor", sDV, tok_xh0 if c < 4 else tok_xh1)
                if key in ln_tp_dv:
                    em.wait("tensor", sAC, ln_tp_dv[key])
                em.op("tensor", "transpose", ps, xhat[par][:, 128 * c:128 * c + 128],
                      identf[:, :], inc=sPE)
                em.wait("scalar", sPE, sPE.n)
                em.op("scalar", "copy",
                      x1hatT[:, SEQ * c + 128 * t:SEQ * c + 128 * t + 128], ps, inc=sAC)
                ln_tp_dv[key] = sAC.n
        em.op("vector", "memset", K_h[64:65, :], 1.0)
        em.op("vector", "memset", K_h2[64:65, :], 1.0, inc=sDV)
        tok_x1hatT = sDV.n
        if s == 0:
            maybe_dump("x1hatT", x1hatT)
            maybe_dump("statsA", stats)
            maybe_dump("xhat0d", xhat[0])
            maybe_dump("xhat1d", xhat[1])

        tok_bv = load_brow(0, bv_d.ap().unsqueeze(0))
        em.wait("tensor", sDV, tok_x1hatT)
        em.wait("tensor", sAC)
        em.wait("tensor", sSET, tok_set_dma)
        em.wait("tensor", sBR[0], tok_bv)
        vp_dv = {}
        for t in range(NT):
            par = t % 2
            psv = ps_a[:, 1024 * par:1024 * par + 1024]
            if t >= 2:
                em.wait("tensor", sDV, vp_dv[par])
            for c in range(NCH):
                lhs = x1hatT[:, SEQ * c + 128 * t:SEQ * c + 128 * t + 128]
                for w in range(2):
                    em.op("tensor", "matmul", psv[:, 512 * w:512 * w + 512],
                          lhs, wvT[:, 1024 * c + 512 * w:1024 * c + 512 * w + 512],
                          start=(c == 0), stop=False)
            for w in range(2):
                em.op("tensor", "matmul", psv[:, 512 * w:512 * w + 512],
                      ones_bf[0:1, :], br[0][0:1, 512 * w:512 * w + 512],
                      start=False, stop=True, inc=(sPE if w == 1 else None))
            em.wait("vector", sPE, sPE.n)
            em.op("vector", "tensor_copy",
                  mk3(V_all, 1040 * t, [[1040 * NT, 128], [65, 16], [1, 64]]),
                  bass.AP(psv.tensor, psv.offset, [[2048, 128], [64, 16], [1, 64]]))
            em.op("vector", "memset",
                  mk3(V_all, 1040 * t + 64, [[1040 * NT, 128], [65, 16], [1, 1]]),
                  1.0, inc=sDV)
            vp_dv[par] = sDV.n
        br_pe[0] = sPE.n
        if s == 0:
            maybe_dump("V", V_all)

        prev_head_dv = sDV.n
        prev_head_pe = sPE.n
        pair_dv = {}
        QKBUF = [(Q_h, K_h), (Q_h2, K_h2)]
        qk_ready = {}

        def emit_quarter(hh, qq):
            dst = QKBUF[hh % 2][0 if qq < 4 else 1]
            wT = wqT if qq < 4 else wkT
            bias_t = bqh if qq < 4 else bkh
            g = qq % 4
            psq = ps_a[0:64, 1024:1536]
            if c1_tok[0]:
                em.wait("tensor", sDV if c1_tok[0][0] == "dv" else sAC, c1_tok[0][1])
            for c in range(NCH):
                em.op("tensor", "matmul", psq,
                      wT[:, 1024 * c + 64 * hh:1024 * c + 64 * hh + 64],
                      x1hatT[:, SEQ * c + 512 * g:SEQ * c + 512 * g + 512],
                      start=(c == 0), stop=(c == NCH - 1),
                      inc=(sPE if c == NCH - 1 else None))
            em.wait("scalar", sPE, sPE.n)
            em.op("scalar", "activation", dst[0:64, 512 * g:512 * g + 512], psq,
                  AF.Identity, bias=bias_t[:, hh:hh + 1], scale=1.0, inc=sAC)
            c1_tok[0] = ("ac", sAC.n)

        c1_tok = [None]
        if s == 0:
            em.wait("scalar", sIN, tok_in_setup)
        em.wait("tensor", sDV)
        for qq in range(8):
            emit_quarter(0, qq)
        qk_ready[0] = sAC.n

        for h in range(NH):
            Qh, Kh = QKBUF[h % 2]
            if s == 0 and h == 0:
                maybe_dump("q0", Qh)
                maybe_dump("k0", Kh)

            em.wait("tensor", sAC, qk_ready[h])
            em.wait("tensor", sDV, prev_head_dv)
            s_dv_tok = {}
            for qt in range(NT):
                par = qt % 2
                psS = ps_a if par == 0 else ps_b
                W = 128 * qt + 128
                if qt >= 2:
                    em.wait("tensor", sDV, s_dv_tok[par])
                lhs = Qh[0:64, 128 * qt:128 * qt + 128]
                for g in range((W + 511) // 512):
                    n = min(512, W - 512 * g)
                    em.op("tensor", "matmul", psS[:, 512 * g:512 * g + n],
                          lhs, Kh[0:64, 512 * g:512 * g + n],
                          start=True, stop=True,
                          inc=(sPE if 512 * g + n >= W else None))
                em.wait("vector", sPE, sPE.n)
                em.op("vector", "reduce_max", negmcols[:, qt:qt + 1], psS[:, 0:W],
                      axis=AX.X, negate=True, inc=sDV)
                s_dv_tok[par] = sDV.n
            tok_negm_dv = sDV.n

            def emit_negm_tp():
                em.wait("tensor", sDV, tok_negm_dv)
                if c1_tok[0]:
                    em.wait("tensor", sDV if c1_tok[0][0] == "dv" else sAC,
                            c1_tok[0][1])
                psT = ps_a[0:16, 1024:1152]
                em.op("tensor", "transpose", psT, negmcols[:, 0:16], identf[:, :],
                      inc=sPE)
                em.wait("scalar", sPE, sPE.n)
                em.op("scalar", "copy", negmT_sb[0:16, :], psT, inc=sAC)
                c1_tok[0] = ("ac", sAC.n)
                em.wait("sync", sAC, sAC.n)
                em.op("sync", "dma_start", out=Qh[64:65, :], in_=negmT_sb[0:16, :],
                      inc=sNEGh[0], incn=16)

            if s == 0 and h == 0:
                maybe_dump("negm0", negmcols)
                maybe_dump("q0r", Qh)
            wins = [(g4, ki) for g4 in range(4) for ki in range(4 * g4 + 4)]
            exp_tok = {}
            av_tok = {}
            o_evac_dv = {}

            def emit_av(p):
                g4, ki, ati = p
                if ki == 0 and g4 > 0:
                    em.wait("tensor", sDV, o_evac_dv[g4 - 1])
                em.wait("tensor", sAC, exp_tok_at[(g4, ki)])
                for qt in range(max(4 * g4, ki), 4 * g4 + 4):
                    qrel = qt - 4 * g4
                    em.op("tensor", "matmul", ps_b[:, 512 * qrel:512 * qrel + 65],
                          AT[ati][:, 128 * qrel:128 * qrel + 128],
                          V_all[:, 1040 * ki + 65 * h:1040 * ki + 65 * h + 65],
                          start=(ki == 0), stop=(ki == qt),
                          inc=(sPE if ki == qt else None))
                av_tok[ati] = sPE.n
                if ki == 4 * g4 + 3:
                    tok_av = sPE.n
                    em.wait("vector", sPE, tok_av)
                    em.op("vector", "reciprocal", stats[:, 8:12],
                          bass.AP(ps_b.tensor, ps_b.offset + 64, [[2048, 128], [512, 4]]),
                          inc=sDV)
                    em.wait("vector", sDV)
                    for qrel in range(4):
                        qt = 4 * g4 + qrel
                        em.op("vector", "tensor_single_scalar",
                              pair[:, 128 * qt + 64 * (h % 2):128 * qt + 64 * (h % 2) + 64],
                              ps_b[:, 512 * qrel:512 * qrel + 64],
                              stats[:, 8 + qrel:9 + qrel], ALU.mult, inc=sDV)
                    o_evac_dv[g4] = sDV.n

            exp_tok_at = {}
            pendq = []
            STOFF = (0, 512, 1536)
            nq = [0]
            if h + 1 < NH:
                em.wait("tensor", sDV, s_dv_tok[0])
                emit_quarter(h + 1, 0)
                emit_quarter(h + 1, 1)
                nq[0] = 2
            emit_negm_tp()
            tok_neg = sNEGh[0].n
            if h + 1 < NH:
                emit_quarter(h + 1, 2)
                emit_quarter(h + 1, 3)
                nq[0] = 4
            for idx, (g4, ki) in enumerate(wins):
                par = idx % 3
                ati = idx % 4
                psW = ps_a[:, STOFF[par]:STOFF[par] + 512]
                off = max(0, 128 * ki - 512 * g4)
                n = 512 - off
                if par in exp_tok:
                    em.wait("tensor", sAC, exp_tok[par])
                if idx == 0:
                    em.wait("tensor", sNEGh[0], tok_neg)
                em.op("tensor", "matmul", psW[:, off:off + n],
                      Kh[0:65, 128 * ki:128 * ki + 128],
                      Qh[0:65, 512 * g4 + off:512 * g4 + off + n],
                      start=True, stop=True, inc=sPE)
                tok_w = sPE.n
                bias_ops = []
                if 4 * g4 <= ki <= 4 * g4 + 3:
                    bias_ops.append((128 * (ki - 4 * g4), 256 * h))
                if 4 * g4 - 1 <= ki <= 4 * g4 + 2:
                    bias_ops.append((128 * (ki + 1 - 4 * g4), 256 * h + 128))
                if bias_ops:
                    em.wait("vector", sPE, tok_w)
                    for bi, (co, po) in enumerate(bias_ops):
                        em.op("vector", "tensor_tensor", psW[:, co:co + 128],
                              psW[:, co:co + 128], PT_bf[:, po:po + 128], ALU.add,
                              inc=(sDV if bi == len(bias_ops) - 1 else None))
                    em.wait("scalar", sDV, sDV.n)
                else:
                    em.wait("scalar", sPE, tok_w)
                if ati in av_tok:
                    em.wait("scalar", sPE, av_tok[ati])
                em.op("scalar", "activation", AT[ati][:, off:off + n],
                      psW[:, off:off + n], AF.Exp, bias=0.0, scale=1.0, inc=sAC)
                exp_tok[par] = sAC.n
                exp_tok_at[(g4, ki)] = sAC.n
                pendq.append((g4, ki, ati))
                if len(pendq) > 3:
                    emit_av(pendq.pop(0))
                if idx % 5 == 4 and h + 1 < NH and nq[0] < 8:
                    emit_quarter(h + 1, nq[0])
                    nq[0] += 1
            for p in pendq:
                emit_av(p)
            if h + 1 < NH:
                while nq[0] < 8:
                    emit_quarter(h + 1, nq[0])
                    nq[0] += 1
                qk_ready[h + 1] = sAC.n
            o_evac_dv = o_evac_dv[3]

            prev_head_dv = o_evac_dv
            prev_head_pe = sPE.n

            if h % 2 == 1:
                ch = h // 2
                em.wait("tensor", sDV, o_evac_dv)
                for qt in range(NT):
                    key = qt % 4
                    ps = ps_b[:, 512 * key:512 * key + 128]
                    if key in pair_dv:
                        em.wait("tensor", sDV, pair_dv[key])
                    em.op("tensor", "transpose", ps, pair[:, 128 * qt:128 * qt + 128],
                          identf[:, :], inc=sPE)
                    em.wait("vector", sPE, sPE.n)
                    em.op("vector", "tensor_copy",
                          attnOT[:, SEQ * ch + 128 * qt:SEQ * ch + 128 * qt + 128], ps,
                          inc=sDV)
                    pair_dv[key] = sDV.n
                prev_head_dv = sDV.n
                prev_head_pe = sPE.n
        tok_attnOT = sDV.n
        if s == 0:
            maybe_dump("attnOT", attnOT)

        tok_bo = load_brow(0, bo_d.ap().unsqueeze(0))
        if s == 0:
            w_build(wo_d, woT, ps_a, store_idx=3)
        else:
            em.wait("sync", sPE, prev_head_pe)
            em.op("sync", "dma_start", out=woT[:, :], in_=wts_d.ap()[3],
                  inc=sWT, incn=16)
            em.wait("tensor", sWT, sWT.n)
        barrier_tensor()
        em.wait("tensor", sBR[0], tok_bo)
        em.wait("sync", sDV)
        d_dv = {}
        ln_tp_dv = {}
        xhat_tok = {}

        def emit_D(t):
            par = t % 2
            psy = ps_a[:, 1024 * par:1024 * par + 1024]
            if t >= 2:
                em.wait("sync", sDV, d_dv[par])
            em.op("sync", "dma_start", out=xin[par][:, :],
                  in_=x[s, 128 * t:128 * t + 128, :], inc=sXI[par], incn=16)
            tok_xr = sXI[par].n
            if t >= 2:
                em.wait("tensor", sDV, d_dv[par])
            for c in range(NCH):
                lhs = attnOT[:, SEQ * c + 128 * t:SEQ * c + 128 * t + 128]
                for w in range(2):
                    em.op("tensor", "matmul", psy[:, 512 * w:512 * w + 512],
                          lhs, woT[:, 1024 * c + 512 * w:1024 * c + 512 * w + 512],
                          start=(c == 0), stop=False)
            for w in range(2):
                em.op("tensor", "matmul", psy[:, 512 * w:512 * w + 512],
                      ones_bf[0:1, :], br[0][0:1, 512 * w:512 * w + 512],
                      start=False, stop=True, inc=(sPE if w == 1 else None))
            return tok_xr, sPE.n

        def emit_D_vec(t, tok_xr, tok_pe):
            par = t % 2
            psy = ps_a[:, 1024 * par:1024 * par + 1024]
            em.wait("vector", sPE, tok_pe)
            em.wait("vector", sXI[par], tok_xr)
            em.op("vector", "tensor_tensor", x2[:, 1024 * t:1024 * t + 1024],
                  xin[par][:, :], psy[:, :], ALU.add, inc=sDV)
            d_dv[par] = sDV.n

        def emit_E(t):
            par = t % 2
            st = 8 * par
            x2t = x2[:, 1024 * t:1024 * t + 1024]
            em.op("vector", "reduce_sum", stats[:, st:st + 1], x2t, axis=AX.X,
                  negate=True, inc=sDV)
            em.op("vector", "tensor_mul", ps_b[:, 0:1024], x2t, x2t)
            em.op("vector", "reduce_sum", stats[:, st + 2:st + 3],
                  ps_b[:, 0:1024], axis=AX.X, inc=sDV)
            em.wait("scalar", sDV, sDV.n)
            if t >= 2:
                em.wait("scalar", sDV, xhat_tok[par])
            em.op("scalar", "activation", stats[:, st + 1:st + 2], stats[:, st:st + 1],
                  AF.Identity, bias=0.0, scale=1.0 / 1024.0)
            em.op("scalar", "activation", stats[:, st + 5:st + 6], stats[:, st:st + 1],
                  AF.Square, bias=0.0, scale=1.0 / 1024.0, inc=sAC)
            em.wait("vector", sAC, sAC.n)
            em.op("vector", "scalar_tensor_tensor", stats[:, st + 6:st + 7],
                  stats[:, st + 2:st + 3], 1.0 / 1024.0, stats[:, st + 5:st + 6],
                  ALU.mult, ALU.subtract, inc=sDV)
            em.wait("scalar", sDV, sDV.n)
            em.op("scalar", "activation", stats[:, st + 4:st + 5], stats[:, st + 6:st + 7],
                  AF.Sqrt, bias=epsc[:, :], scale=1.0, inc=sAC)
            em.wait("vector", sAC, sAC.n)
            em.op("vector", "reciprocal", stats[:, st + 3:st + 4], stats[:, st + 4:st + 5],
                  inc=sDV)
            em.wait("vector", sDV)
            em.op("vector", "tensor_scalar", xhat[par][:, 0:512], x2t[:, 0:512],
                  stats[:, st + 1:st + 2], stats[:, st + 3:st + 4], ALU.add, ALU.mult,
                  inc=sDV)
            tok_xh0 = sDV.n
            em.op("vector", "tensor_scalar", xhat[par][:, 512:1024], x2t[:, 512:1024],
                  stats[:, st + 1:st + 2], stats[:, st + 3:st + 4], ALU.add, ALU.mult,
                  inc=sDV)
            xhat_tok[par] = sDV.n
            tok_xh1 = sDV.n
            for c in range(NCH):
                key = (par, c % 2)
                ps = ps_b[:, 1024 + 512 * (c % 2):1024 + 512 * (c % 2) + 128]
                em.wait("tensor", sDV, tok_xh0 if c < 4 else tok_xh1)
                if key in ln_tp_dv:
                    em.wait("tensor", sAC, ln_tp_dv[key])
                em.op("tensor", "transpose", ps, xhat[par][:, 128 * c:128 * c + 128],
                      identf[:, :], inc=sPE)
                em.wait("scalar", sPE, sPE.n)
                em.op("scalar", "copy",
                      x2hatT[:, SEQ * c + 128 * t:SEQ * c + 128 * t + 128], ps, inc=sAC)
                ln_tp_dv[key] = sAC.n

        if upto == "attn":
            for t in range(NT):
                tok_xr, tok_pe = emit_D(t)
                emit_D_vec(t, tok_xr, tok_pe)
            em.wait("sync", sDV)
            for t in range(NT):
                em.op("sync", "dma_start", out=y[s, 128 * t:128 * t + 128, :],
                      in_=x2[:, 1024 * t:1024 * t + 1024], inc=sST, incn=16)
            br_pe[1] = sPE.n
            for _i in range(3):
                wbuf_free[_i] = max(wbuf_free.get(_i, 0), sPE.n)
            continue

        for t in range(NT + 1):
            if t < NT:
                tok_xr, tok_pe = emit_D(t)
            if t >= 1:
                emit_E(t - 1)
            if t < NT:
                emit_D_vec(t, tok_xr, tok_pe)
        tok_x2 = sDV.n
        br_pe[1] = sPE.n
        for _i in range(3):
            wbuf_free[_i] = max(wbuf_free.get(_i, 0), sPE.n)
        tok_x2hatT = sAC.n

        em.wait("tensor", sAC, tok_x2hatT)
        rt_dv = {}
        cw_tok = 0
        cw_pend = []

        def emit_cwT(p):
            tt, tok = p
            parr = tt % 2
            psT2 = ps_a[0:15, 512 * parr + 256:512 * parr + 384]
            em.wait("tensor", sDV, tok)
            em.op("tensor", "transpose", psT2, cw[:, 15 * tt:15 * tt + 15],
                  identf[:, :], inc=sPE)
            em.wait("vector", sPE, sPE.n)
            em.op("vector", "tensor_copy", cwT[0:15, 128 * tt:128 * tt + 128], psT2,
                  inc=sDV)
            rt_dv[parr] = max(rt_dv.get(parr, 0), sDV.n)

        for t in range(NT):
            par = t % 2
            psl = ps_a[:, 512 * par:512 * par + 15]
            if par in rt_dv:
                em.wait("tensor", sDV, rt_dv[par])
            for c in range(NCH):
                em.op("tensor", "matmul", psl,
                      x2hatT[:, SEQ * c + 128 * t:SEQ * c + 128 * t + 128],
                      rW[:, 15 * c:15 * c + 15], start=(c == 0), stop=False)
            em.op("tensor", "matmul", psl, ones_bf[0:1, :], brow_rt[0:1, 0:15],
                  start=False, stop=True, inc=sPE)
            em.wait("vector", sPE, sPE.n)
            em.op("vector", "reduce_max", stats[:, 8:9], psl, axis=AX.X, negate=True,
                  inc=sDV)
            em.wait("scalar", sDV, sDV.n)
            if cw_tok:
                em.wait("scalar", sDV, cw_tok)
            em.op("scalar", "activation", probs[:, 0:15], psl, AF.Exp,
                  bias=stats[:, 8:9], scale=1.0, accum_out=stats[:, 9:10], inc=sAC)
            em.wait("vector", sAC, sAC.n)
            em.op("vector", "reciprocal", stats[:, 10:11], stats[:, 9:10], inc=sDV)
            em.wait("vector", sDV)
            em.op("vector", "tensor_single_scalar", probs[:, 0:15], probs[:, 0:15],
                  stats[:, 10:11], ALU.mult, inc=sDV)
            em.wait("vector", sDV)
            em.op("vector", "max", top8[:, :], probs[:, 0:15], inc=sDV)
            em.wait("vector", sDV)
            em.op("vector", "tensor_single_scalar", msk15[:, 0:15], probs[:, 0:15],
                  top8[:, 2:3], ALU.is_ge, inc=sDV)
            em.wait("vector", sDV)
            em.op("vector", "tensor_tensor", cw[:, 15 * t:15 * t + 15], msk15[:, 0:15],
                  probs[:, 0:15], ALU.mult, inc=sDV)
            rt_dv[par] = max(rt_dv.get(par, 0), sDV.n)
            cw_tok = sDV.n
            cw_pend.append((t, sDV.n))
            if len(cw_pend) > 1:
                emit_cwT(cw_pend.pop(0))
        for p in cw_pend:
            emit_cwT(p)

        n_exp = 1 if upto == "shared" else (NE + 1)
        for e in range(n_exp):
            gi = s * (NE + 1) + e
            b1 = (2 * gi) % 3
            b2 = (2 * gi + 1) % 3
            if e == 0:
                src1 = bass.AP(sw1_d, 0, [[1024, 128], [131072, 8], [1, 1024]])
                src2 = bass.AP(sw2_d, 0, [[1024, 128], [131072, 8], [1, 1024]])
            else:
                ee = e - 1
                src1 = bass.AP(ew1_d, ee * 1048576, [[1024, 128], [131072, 8], [1, 1024]])
                src2 = bass.AP(ew2_d, ee * 1048576, [[1024, 128], [131072, 8], [1, 1024]])
            for (bi, src) in ((b1, src1), (b2, src2)):
                em.wait("gpsimd", sPE, wbuf_free.get(bi, 0))
                em.op("gpsimd", "dma_start", out=wbuf[bi][:, :], in_=src,
                      inc=sWD[bi], incn=16)
            tok_w1 = sWD[b1].n
            tok_w2 = sWD[b2].n
            tok_b2row = load_brow(0, sb2_d.ap().unsqueeze(0)) if e == 0 else 0
            barrier_tensor()
            em.wait("tensor", sWD[b1], tok_w1)
            if gi > 0:
                em.wait("scalar", sPE, w2_pe_tok)
            h_evac = {}
            for j in range(NCH):
                par = j % 2
                psh = ps_a if par == 0 else ps_b
                if par in h_evac:
                    em.wait("tensor", sAC, h_evac[par])
                for c in range(NCH):
                    lhs = wbuf[b1][:, 1024 * c + 128 * j:1024 * c + 128 * j + 128]
                    for g in range(4):
                        em.op("tensor", "matmul", psh[:, 512 * g:512 * g + 512],
                              lhs, x2hatT[:, SEQ * c + 512 * g:SEQ * c + 512 * g + 512],
                              start=(c == 0), stop=(c == NCH - 1),
                              inc=(sPE if (c == NCH - 1 and g == 3) else None))
                em.wait("scalar", sPE, sPE.n)
                if e == 0:
                    bias_ap = sb1c[:, j:j + 1]
                else:
                    bias_ap = eb1c[:, 8 * (e - 1) + j:8 * (e - 1) + j + 1]
                em.op("scalar", "activation", htil[:, SEQ * j:SEQ * j + SEQ],
                      psh[:, :], AF.Relu, bias=bias_ap, scale=1.0, inc=sAC)
                h_evac[par] = sAC.n
            tok_hdone = sAC.n
            wbuf_free[b1] = 0
            em.wait("tensor", sAC, tok_hdone)
            em.wait("tensor", sWD[b2], tok_w2)
            if e == 0:
                em.wait("tensor", sBR[0], tok_b2row)
            y_dv = {}
            for t in range(NT):
                par = t % 2
                psy = ps_a[:, 1024 * par:1024 * par + 1024]
                if par in y_dv:
                    em.wait("tensor", sDV, y_dv[par])
                for c in range(NCH):
                    lhs = htil[:, SEQ * c + 128 * t:SEQ * c + 128 * t + 128]
                    last = (c == NCH - 1 and e != 0)
                    for w in range(2):
                        em.op("tensor", "matmul", psy[:, 512 * w:512 * w + 512],
                              lhs, wbuf[b2][:, 1024 * c + 512 * w:1024 * c + 512 * w + 512],
                              start=(c == 0), stop=last,
                              inc=(sPE if (last and w == 1) else None))
                if e == 0:
                    for w in range(2):
                        em.op("tensor", "matmul", psy[:, 512 * w:512 * w + 512],
                              ones_bf[0:1, :], br[0][0:1, 512 * w:512 * w + 512],
                              start=False, stop=False)
                    for w in range(2):
                        em.op("tensor", "matmul", psy[:, 512 * w:512 * w + 512],
                              cwT[0:15, 128 * t:128 * t + 128],
                              eb2a[0:15, 512 * w:512 * w + 512],
                              start=False, stop=True, inc=(sPE if w == 1 else None))
                em.wait("vector", sPE, sPE.n)
                x2t = x2[:, 1024 * t:1024 * t + 1024]
                if e == 0:
                    em.op("vector", "tensor_tensor", x2t, x2t, psy[:, :], ALU.add, inc=sDV)
                else:
                    em.op("vector", "scalar_tensor_tensor", x2t, psy[:, :],
                          cw[:, 15 * t + e - 1:15 * t + e], x2t, ALU.mult, ALU.add,
                          inc=sDV)
                y_dv[par] = sDV.n
                if e == n_exp - 1:
                    em.wait("sync", sDV, sDV.n)
                    em.op("sync", "dma_start", out=y[s, 128 * t:128 * t + 128, :],
                          in_=x2t, inc=sST, incn=16)
            w2_pe_tok = sPE.n
            wbuf_free[b1] = sPE.n
            wbuf_free[b2] = sPE.n
            if e == 0:
                br_pe[0] = sPE.n

    em.wait("sync", sST, sST.n)

    with nc.Block() as block:
        @block.tensor
        def _(e):
            for f in em.q["tensor"]:
                f(e)

        @block.vector
        def _(e):
            for f in em.q["vector"]:
                f(e)

        @block.scalar
        def _(e):
            for f in em.q["scalar"]:
                f(e)

        @block.gpsimd
        def _(e):
            for f in em.q["gpsimd"]:
                f(e)

        @block.sync
        def _(e):
            for f in em.q["sync"]:
                f(e)

    return nc


def kernel(**inputs):
    x = np.ascontiguousarray(inputs["L0_attn_x"])
    nc = build_nc()
    base = {
        "wq": np.ascontiguousarray(inputs["L0_attn_Wq"]),
        "wk": np.ascontiguousarray(inputs["L0_attn_Wk"]),
        "wv": np.ascontiguousarray(inputs["L0_attn_Wv"]),
        "wo": np.ascontiguousarray(inputs["L0_attn_Wo"]),
        "bq": np.ascontiguousarray(inputs["L0_attn_bq"]),
        "bk": np.ascontiguousarray(inputs["L0_attn_bk"]),
        "bv": np.ascontiguousarray(inputs["L0_attn_bv"]),
        "bo": np.ascontiguousarray(inputs["L0_attn_bo"]),
        "re": np.ascontiguousarray(inputs["L0_attn_rel_emb"]),
        "rw": np.ascontiguousarray(inputs["L0_ffn_router_W"]),
        "rb": np.ascontiguousarray(inputs["L0_ffn_router_b"]),
        "sw1": np.ascontiguousarray(inputs["L0_ffn_sW1"]),
        "sb1": np.ascontiguousarray(inputs["L0_ffn_sb1"]),
        "sw2": np.ascontiguousarray(inputs["L0_ffn_sW2"]),
        "sb2": np.ascontiguousarray(inputs["L0_ffn_sb2"]),
        "ew1": np.ascontiguousarray(inputs["L0_ffn_eW1"]),
        "eb1": np.ascontiguousarray(inputs["L0_ffn_eb1"]),
        "ew2": np.ascontiguousarray(inputs["L0_ffn_eW2"]),
        "eb2": np.ascontiguousarray(inputs["L0_ffn_eb2"]),
    }
    in_maps = []
    for c in range(8):
        m = dict(base)
        m["x"] = np.ascontiguousarray(x[2 * c:2 * c + 2])
        in_maps.append(m)
    res = run_bass_kernel_spmd(nc, in_maps, core_ids=list(range(8)))
    return np.concatenate([r["y"] for r in res.results], axis=0)
